# version 02
# baseline (speedup 1.0000x reference)
import numpy as np
import ml_dtypes
import concourse.bass as bass
import concourse.mybir as mybir
from concourse.bass_utils import run_bass_kernel_spmd

F32 = mybir.dt.float32
BF16 = mybir.dt.bfloat16
AF = mybir.ActivationFunctionType
ALU = mybir.AluOpType
AX = mybir.AxisListType
BF = ml_dtypes.bfloat16

N = 4096
D = 1024
NT = N // 128
NDC = D // 128
FF = 4096
FQ = 1024
NFQ = FQ // 128
NQ = FF // FQ
NEXP = 8
TT = 1024
NTILE = N // TT
NSUB = TT // 128
NTH = TT // 512
G1 = NFQ * NTH
G2 = NSUB * 2
NHEAD = 16
DH = 64

EPS0 = 1e-5
EPSQK = 1e-6
EPSG = 1e-3
ATT_SCALE = 0.125

class SC:
    def __init__(self, name, step=1):
        self.name, self.step, self.h, self.cur = name, step, None, 0


class MockIns:
    def then_inc(self, h, n):
        return self


class Eng:

    def __init__(self, eng, real, auto_drain=False):
        self.e, self.real, self.auto_drain = eng, real, auto_drain

    def wait(self, sc, v):
        if self.real:
            self.e.wait_ge(sc.h, v)

    def inc(self, ins, sc, n=1):
        sc.cur += n * sc.step
        if self.real:
            ins.then_inc(sc.h, n * sc.step)
        return ins

    def op(self, name, *a, nodrain=False, **kw):
        if self.real:
            r = getattr(self.e, name)(*a, **kw)
            if self.auto_drain and not nodrain and name != "dma_start":
                self.e.drain()
            return r
        return MockIns()

    def dr(self):
        if self.real:
            self.e.drain()

    def incd(self, sc, n=1):
        sc.cur += n * sc.step
        if self.real:
            self.e.drain()
            self.e.sem_inc(sc.h, n * sc.step)

    def __getattr__(self, name):
        def call(*a, **kw):
            return self.op(name, *a, **kw)
        return call

    def done(self, P):
        P.pb.cur += 1
        if self.real:
            self.e.drain()
            self.e.sem_inc(P.pb.h, 1)


SEMS = [("ld", 16), ("wl", 16), ("st", 16),
        ("gwl", 16), ("wlA", 16), ("wlB", 16), ("b2l", 16), ("ldE", 16),
        ("tp", 1), ("te", 1), ("pg", 1), ("gc", 1), ("ge", 1), ("gv", 1), ("aff", 1),
        ("pe1", 1), ("ev1", 1), ("pe2", 1), ("ev2", 1),
        ("ax", 1), ("ep", 1), ("fin", 1), ("pb", 1)]

ENGINES = ("gpsimd", "tensor", "scalar", "vector", "sync")

SB_BASE = 16512
SB_STAT = 0
SB_GATES = 1536
SB_MISC = 2560
SB_XF = 8 * 1024
SB_XE = 24 * 1024
SB_H = 40 * 1024
SB_Y = 56 * 1024
SB_WA = 88 * 1024
SB_WB = 120 * 1024
SB_FREE = 152 * 1024
SB_END = 212832


class Prog:
    def __init__(self):
        self.nc = bass.Bass("TRN2", target_bir_lowering=False,
                            dynamic_dma_scratch_size=16384)
        self.sc = {}
        for nm, st in SEMS:
            s = SC(nm, st)
            self.sc[nm] = s
            setattr(self, nm, s)
        self.dram = {}
        self.sb = {}
        self.ps = {}
        self.consts = {}

    def din(self, name, shape, dtype=F32):
        self.dram[name] = self.nc.dram_tensor(name, list(shape), dtype,
                                              kind="ExternalInput").ap()
        return self.dram[name]

    def dout(self, name, shape, dtype=F32):
        self.dram[name] = self.nc.dram_tensor(name, list(shape), dtype,
                                              kind="ExternalOutput").ap()
        return self.dram[name]

    def dtmp(self, name, shape, dtype=F32):
        self.dram[name] = self.nc.dram_tensor(name, list(shape), dtype,
                                              kind="Internal").ap()
        return self.dram[name]

    def const(self, name, arr):
        if name not in self.consts:
            self.consts[name] = self.nc.inline_tensor(
                np.ascontiguousarray(arr), name=name).ap()
        return self.consts[name]

    def sbuf(self, name, off, shape, dtype):
        if name not in self.sb:
            self.sb[name] = self.nc.alloc_sbuf_tensor_at(
                name, list(shape), dtype, offset=SB_BASE + off).ap()
        return self.sb[name]

    def fb(self, name, shape, dtype):
        import math
        nbytes = int(np.prod(shape[1:])) * (2 if dtype == BF16 else 4)
        nbytes = (nbytes + 511) // 512 * 512
        cur = getattr(self, "_fcur", 0)
        assert SB_FREE + cur + nbytes <= SB_END, (name, cur, nbytes)
        ap = self.sbuf(name, SB_FREE + cur, shape, dtype)
        self._fcur = cur + nbytes
        return ap

    def fb_at(self, name, off, shape, dtype):
        return self.sbuf(name, SB_FREE + off, shape, dtype)

    def cf(self, value):
        if not hasattr(self, "cfvals"):
            self.cfvals = []
        if value not in self.cfvals:
            self.cfvals.append(value)
        i = self.cfvals.index(value)
        assert i < 8
        return self.sb["cst"][:, i:i + 1]


class Phase:
    name = "ph"

    def __init__(self, P):
        self.P = P
        self.idx = None
        self.bases = {}

    def setup(self):
        pass

    def b(self, nm):
        return self.bases[nm]

    def emit(self, eng_name, E):
        E.wait(self.P.pb, 5 * self.idx)
        fn = getattr(self, "e_" + eng_name, None)
        if fn is not None:
            fn(E)
        E.done(self.P)


def build_program(phases_cls, mode_io):
    P = Prog()
    P.mode_io = mode_io
    declare_tensors(P)
    phases = []
    for f in phases_cls:
        ph = f(P)
        phases.append(ph)
    for ph in phases:
        ph.setup()
    for i, ph in enumerate(phases):
        ph.idx = i
        ph.bases = {nm: P.sc[nm].cur for nm in P.sc}
        for en in ENGINES:
            ph.emit(en, Eng(None, False))
    totals = {nm: P.sc[nm].cur for nm in P.sc}
    for nm in P.sc:
        P.sc[nm].cur = 0
    nc = P.nc
    import contextlib
    with contextlib.ExitStack() as stack:
        for nm, _ in SEMS:
            P.sc[nm].h = stack.enter_context(nc.semaphore(nm))
        with nc.Block() as blk:
            def mk(en):
                def run(eng):
                    ad = en in ("vector",)
                    for ph in phases:
                        ph.emit(en, Eng(eng, True, auto_drain=ad))
                return run
            blk.gpsimd(mk("gpsimd"))
            blk.tensor(mk("tensor"))
            blk.scalar(mk("scalar"))
            blk.vector(mk("vector"))
            blk.sync(mk("sync"))
    for nm in P.sc:
        assert P.sc[nm].cur == totals[nm], (nm, P.sc[nm].cur, totals[nm])
    return P


def declare_tensors(P):
    io = getattr(P, "mode_io", {})

    def scratch(name, shape, dtype):
        k = io.get(name)
        if k == "in":
            return P.din(name, shape, dtype)
        if k == "out":
            return P.dout(name, shape, dtype)
        return P.dtmp(name, shape, dtype)

    P.din("x_in", [128, NT, D], F32)
    for w in ("wq", "wk", "wv", "wq2", "wk2", "wv2", "wo"):
        P.din(w, [128, NDC, D], F32)
    P.din("bqkv1", [128, 3, 64], F32)
    P.din("rows0", [1, 4, D], F32)
    P.din("g0rep", [128, D], F32)
    P.din("be0rep", [128, D], F32)
    P.din("gw0", [128, NDC, 32], F32)
    P.din("w1e0", [NEXP, NQ, 128, NDC, FQ], F32)
    P.din("b1e0", [NEXP, NQ, 128, 128], F32)
    P.din("w2e0", [NEXP, NQ, 128, NFQ, D], F32)
    P.din("b2e0", [NEXP, D], F32)
    P.din("wqkv", [128, NDC, 3 * D], F32)
    P.din("l1grep", [128, D], F32)
    P.din("l1brep", [128, D], F32)
    P.din("bqkv", [128, 128], F32)
    P.din("wout1", [128, NDC, D], F32)
    P.din("rows1", [1, 8, D], F32)
    P.din("g2rep", [128, D], F32)
    P.din("b2rep", [128, D], F32)
    P.din("b2mrow", [1, D], F32)
    P.din("w1m", [NQ, 128, NDC, FQ], F32)
    P.din("b1m", [NQ, 128, 128], F32)
    P.din("w2m", [NQ, 128, NFQ, D], F32)
    P.din("gw1", [128, NDC, 32], F32)
    P.din("rb1", [128, 128], F32)
    P.din("lng", [128, NEXP, 16], F32)
    P.din("lnb", [128, NEXP, 16], F32)
    P.din("w1e1", [NEXP, NQ, 128, NDC, FQ], F32)
    P.din("b1e1", [NEXP, NQ, 128, 128], F32)
    P.din("w2e1", [NEXP, NQ, 128, NFQ, D], F32)
    P.din("b2e1", [NEXP, D], F32)
    scratch("xn0", [128, NT, D], BF16)
    scratch("y0", [128, NT, D], F32)
    scratch("ved", [128, 8, 4, D], BF16)
    scratch("x1", [128, NT, D], F32)
    scratch("xh1", [128, NT, D], BF16)
    scratch("y1", [128, NT, D], F32)
    scratch("xa", [128, NT, D], F32)
    scratch("stats", [128, 12, NT], F32)
    scratch("gdump", [128, NT, 8], F32)
    scratch("ldump", [128, NT, 8], F32)
    scratch("fdump", [128, NTILE, NDC, TT], BF16)
    scratch("sdump", [128, 512], F32)
    scratch("adump", [128, NG2 if False else 8, NDC, 512], BF16)
    scratch("edump", [128, 3, D], F32)
    scratch("tdump", [128, NTILE, NSUB, D], BF16)
    scratch("pdump", [128, 1024], BF16)
    P.dout("out", [128, NT, D], F32)

    P.sbuf("stat", SB_STAT, [128, 12, NT], F32)
    P.sbuf("ones_bf", SB_MISC, [1, 512], BF16)
    P.sbuf("ones_f32", SB_MISC + 1024, [1, 128], F32)
    P.sbuf("ident", SB_MISC + 1536, [128, 256], BF16)
    P.sbuf("cst", SB_MISC + 2048, [128, 8], F32)
    nc = P.nc
    P.ps["psA"] = nc.alloc_psum_tensor("psA", [128, 512], F32).ap()
    P.ps["psB"] = nc.alloc_psum_tensor("psB", [128, 512], F32).ap()
    P.ps["psC"] = nc.alloc_psum_tensor("psC", [128, 1024], F32).ap()
    P.ps["psD"] = nc.alloc_psum_tensor("psD", [128, 1024], F32).ap()
    P.ps["pT0"] = nc.alloc_psum_tensor("pT0", [128, 1024], BF16).ap()
    P.ps["pT1"] = nc.alloc_psum_tensor("pT1", [128, 1024], BF16).ap()


ST_MU0, ST_R0, ST_NB0 = 0, 1, 2
ST_R1, ST_NB1 = 3, 4
ST_RA, ST_NBA, ST_SA = 5, 6, 7
ST_RXA, ST_NBXA = 8, 9


def stat(P, row, n):
    return P.sb["stat"][:, row, n:n + 1]


class PH0(Phase):
    name = "ph0"

    def setup(self):
        P = self.P
        self.xt = P.sbuf("ph0_xt", SB_FREE, [128, 2, D], F32)
        self.sq = P.sbuf("ph0_sq", SB_FREE + 8192, [128, D], F32)
        self.xn = P.sbuf("ph0_xn", SB_FREE + 12288, [128, 2, D], BF16)
        self.t = P.sbuf("ph0_t", SB_FREE + 16384, [128, 16], F32)
        self.c_ones_bf = P.const("c_ones_bf", np.ones((1, 512), dtype=BF))
        self.c_ones_f = P.const("c_ones_f", np.ones((1, 128), dtype=np.float32))
        self.c_ident = P.const("c_ident", np.concatenate([np.eye(128, dtype=BF), np.zeros((128, 128), BF)], 1))

    def e_sync(self, E):
        P = self.P
        E.inc(E.dma_start(out=P.sb["ones_bf"][:], in_=self.c_ones_bf), P.ld)
        E.inc(E.dma_start(out=P.sb["ones_f32"][:], in_=self.c_ones_f), P.ld)
        E.inc(E.dma_start(out=P.sb["ident"][:], in_=self.c_ident), P.ld)
        for n in range(NT):
            if n >= 2:
                E.wait(P.ev2, self.b("ev2") + (n - 1))
            E.inc(E.dma_start(out=self.xt[:, n % 2, :],
                              in_=P.dram["x_in"][:, n, :]), P.ld)
            E.wait(P.ev2, self.b("ev2") + n + 1)
            E.inc(E.dma_start(out=P.dram["xn0"][:, n, :],
                              in_=self.xn[:, n % 2, :]), P.st)
        E.wait(P.fin, self.b("fin") + NT)
        E.inc(E.dma_start(out=P.dram["stats"][:, 0:3, :],
                          in_=P.sb["stat"][:, 0:3, :]), P.st)
        E.wait(P.st, self.b("st") + 16 * (NT + 1))

    def e_vector(self, E):
        P = self.P
        t = self.t
        for i, v in enumerate(P.cfvals):
            ins = E.op("memset", P.sb["cst"][:, i:i + 1], v)
        E.inc(ins, P.gc)
        for n in range(NT):
            i = n % 2
            o = 8 * i
            E.wait(P.ld, self.b("ld") + 16 * (n + 4))
            if n >= 2:
                E.wait(P.ev1, self.b("ev1") + n - 1)
            E.op("reduce_sum", t[:, o:o + 1], self.xt[:, i, :], axis=AX.X)
            E.op("tensor_mul", self.sq[:], self.xt[:, i, :], self.xt[:, i, :])
            E.op("reduce_sum", t[:, o + 1:o + 2], self.sq[:], axis=AX.X)
            E.op("tensor_scalar_mul", stat(P, ST_MU0, n), t[:, o:o + 1], 1.0 / D)
            E.op("tensor_scalar_mul", t[:, o + 2:o + 3], t[:, o + 1:o + 2], 1.0 / D)
            E.op("tensor_mul", t[:, o + 3:o + 4], stat(P, ST_MU0, n),
                 stat(P, ST_MU0, n))
            E.inc(E.op("tensor_sub", t[:, o + 4:o + 5], t[:, o + 2:o + 3],
                       t[:, o + 3:o + 4]), P.ax)
            E.wait(P.ev1, self.b("ev1") + n + 1)
            E.op("reciprocal", stat(P, ST_R0, n), t[:, o + 6:o + 7])
            E.inc(E.op("tensor_scalar", stat(P, ST_NB0, n), stat(P, ST_MU0, n),
                       stat(P, ST_R0, n), -1.0, ALU.mult, ALU.mult), P.fin)

    def e_scalar(self, E):
        P = self.P
        t = self.t
        for n in range(NT):
            o = 8 * (n % 2)
            E.wait(P.ax, self.b("ax") + n + 1)
            if n == 0:
                E.wait(P.gc, self.b("gc") + 1)
            E.inc(E.op("activation", t[:, o + 6:o + 7], t[:, o + 4:o + 5], AF.Sqrt,
                       bias=P.cf(EPS0)), P.ev1)
            E.wait(P.fin, self.b("fin") + n + 1)
            if n >= 2:
                E.wait(P.st, self.b("st") + 16 * n)
            E.inc(E.op("activation", self.xn[:, n % 2, :], self.xt[:, n % 2, :],
                       AF.Identity, scale=stat(P, ST_R0, n),
                       bias=stat(P, ST_NB0, n)), P.ev2)



class MoE(Phase):

    def __init__(self, P, kind):
        super().__init__(P)
        self.kind = kind
        self.name = "moe" + kind
        self.nexp = 1 if kind == "mlp" else NEXP
        self.act = AF.Relu if kind == "l0" else AF.Gelu
        self.has_gates = kind in ("l0", "l1")

    def setup(self):
        P = self.P
        k = self.kind
        pre = self.name
        self.in_d = {"l0": P.dram["xn0"], "l1": P.dram["xh1"],
                     "mlp": P.dram["xa"]}[k]
        self.out_d = {"l0": P.dram["y0"], "l1": P.dram["y1"],
                      "mlp": P.dram["out"]}[k]
        wn = {"l0": ("w1e0", "b1e0", "w2e0", "b2e0"),
              "l1": ("w1e1", "b1e1", "w2e1", "b2e1"),
              "mlp": ("w1m", "b1m", "w2m", "b2mrow")}[k]
        self.w1_d, self.b1_d, self.w2_d = (P.dram[wn[0]], P.dram[wn[1]],
                                           P.dram[wn[2]])
        self.b2_d = P.dram[wn[3]]
        self.xf = P.sbuf(pre + "_xf", SB_XF, [128, NDC, TT], BF16)
        self.y = P.sbuf(pre + "_y", SB_Y, [128, NSUB, D], F32)
        self.w1s = P.sbuf(pre + "_w1s", SB_WA, [128, 2, NDC, FQ], BF16)
        self.w2s = P.sbuf(pre + "_w2s", SB_WB, [128, 2, NFQ, D], BF16)
        P._fcur = 0
        self.b1s = P.fb(pre + "_b1s", [128, 2, 128], F32)
        self.b2s = P.fb(pre + "_b2s", [1, 1 if k == "mlp" else 2, D], BF16)
        if k == "mlp":
            self.xa_t = P.sbuf(pre + "_xat", SB_XE, [128, NSUB, D], F32)
            self.h = P.fb(pre + "_h", [128, NFQ, TT], BF16)
            self.xt = P.fb(pre + "_xt", [128, NSUB, D], BF16)
            xt_off = P._fcur - 16384
            self.g2b = P.fb(pre + "_g2b", [128, D], F32)
            self.b2b = P.fb(pre + "_b2b", [128, D], F32)
            self.eb = P.fb(pre + "_eb", [128, 2, D], F32)
            self.et = P.fb_at(pre + "_et", xt_off, [128, 2, D], F32)
            self.eb2 = P.fb_at(pre + "_eb2", xt_off + 8192, [128, 2, D], F32)
        else:
            self.xe = P.sbuf(pre + "_xe", SB_XE, [128, NDC, TT], BF16)
            self.h = P.sbuf(pre + "_h", SB_H, [128, NFQ, TT], BF16)
            self.xt = P.fb(pre + "_xt", [128, NSUB, D], BF16)
            self.gw = P.fb(pre + "_gw", [128, NDC, 32], BF16)
            self.gs = P.sbuf(pre + "_gs", SB_GATES, [128, NT, 8], F32)
            self.lgr = P.fb(pre + "_lgr", [128, 8], F32)
            self.gt = P.fb(pre + "_gt", [128, 16], F32)
            if k == "l1":
                self.lngs = P.fb(pre + "_lng", [128, NEXP, 16], F32)
                self.lnbs = P.fb(pre + "_lnb", [128, NEXP, 16], F32)
                self.rbbf = P.fb(pre + "_rbb", [128, 128], F32)
                self.eb = P.fb(pre + "_eb", [128, 2, D], F32)
                self.et = P.fb(pre + "_et", [128, 2, D], F32)
        if P.mode_io.get("ldump") == "out" and k != "mlp":
            self.ldmp = P.fb(pre + "_ldmp", [128, NT, 8], F32)

    def nst(self):
        return NTILE * self.nexp * NQ

    def wpre(self):
        return 1

    def wl_cum(self, s):
        b2n = 0 if self.kind == "mlp" else (s + NQ - 1) // NQ
        return self.wpre() + 3 * s + b2n

    def wl_w1(self, s):
        b2 = 1 if (s % NQ == 0 and self.kind != "mlp") else 0
        return 16 * (self.wl_cum(s) + b2 + 2)

    def wl_w2(self, s):
        b2 = 1 if (s % NQ == 0 and self.kind != "mlp") else 0
        return 16 * (self.wl_cum(s) + b2 + 3)

    def wl_b2(self, ge):
        if self.kind == "mlp":
            return 16 * 1
        s = ge * NQ
        return 16 * (self.wl_cum(s) + 1)

    def lpre(self):
        return {"l0": 0, "l1": 3, "mlp": 2}[self.kind]

    def ld_tile(self):
        return 1

    def ld_in(self, t):
        return self.b("ld") + 16 * (self.lpre() + t + 1)

    def ld_epi(self, t, sub, nper):
        return self.b("ldE") + 16 * nper * (t * NSUB + sub + 1)

    def e_gpsimd(self, E):
        P = self.P
        if self.has_gates:
            E.inc(E.dma_start(out=self.gw[:],
                              in_=P.dram["gw0" if self.kind == "l0" else "gw1"]), P.gwl)
        else:
            E.inc(E.dma_start(out=self.b2s[0:1, 0, :], in_=self.b2_d), P.b2l)
        for s in range(self.nst()):
            ge = s // NQ
            e = ge % self.nexp
            q = s % NQ
            slot = s % 2
            if q == 0 and self.kind != "mlp":
                if ge >= 2:
                    E.wait(P.pe2, self.b("pe2") + (ge - 1) * NQ * G2 + G2)
                E.inc(E.dma_start(out=self.b2s[0:1, ge % 2, :],
                                  in_=self.b2_d[e:e + 1, :]), P.b2l)
            if s >= 2:
                E.wait(P.pe1, self.b("pe1") + (s - 1) * G1 + 1)
            mlp = self.kind == "mlp"
            E.inc(E.dma_start(out=self.w1s[:, slot],
                              in_=self.w1_d[q] if mlp else self.w1_d[e, q]), P.wlA)
            E.inc(E.dma_start(out=self.b1s[:, slot],
                              in_=self.b1_d[q] if mlp else self.b1_d[e, q]), P.wlA)
            if s >= 2:
                E.wait(P.pe2, self.b("pe2") + (s - 1) * G2 + 1)
            E.inc(E.dma_start(out=self.w2s[:, slot],
                              in_=self.w2_d[q] if mlp else self.w2_d[e, q]), P.wlB)

    def e_sync(self, E):
        P = self.P
        k = self.kind
        if k == "l1":
            E.inc(E.dma_start(out=self.rbbf[:], in_=P.dram["rb1"]), P.ld)
            E.inc(E.dma_start(out=self.lngs[:], in_=P.dram["lng"]), P.ld)
            E.inc(E.dma_start(out=self.lnbs[:], in_=P.dram["lnb"]), P.ld)
        elif k == "mlp":
            E.inc(E.dma_start(out=self.g2b[:], in_=P.dram["g2rep"]), P.ld)
            E.inc(E.dma_start(out=self.b2b[:], in_=P.dram["b2rep"]), P.ld)
        for t in range(NTILE):
            if t >= 1:
                E.wait(P.tp, self.b("tp") + t * NSUB)
                if k == "mlp":
                    E.wait(P.fin, self.b("fin") + t * NSUB)
            tgt = self.xa_t if k == "mlp" else self.xt
            E.inc(E.dma_start(out=tgt[:],
                              in_=self.in_d[:, t * NSUB:(t + 1) * NSUB, :]), P.ld)
            if k == "l0":
                E.wait(P.ev2, self.b("ev2") + (t + 1) * self.nexp * NQ * G2)
            else:
                src1 = P.dram["x1"] if k == "l1" else P.dram["y1"]
                E.wait(P.tp, self.b("tp") + (t + 1) * NSUB)
                for sub in range(NSUB):
                    j = t * NSUB + sub
                    if j >= 2:
                        E.wait(P.fin, self.b("fin") + j - 1)
                    if j >= 1:
                        E.wait(P.ep, self.b("ep") + j)
                    E.inc(E.dma_start(out=self.eb[:, sub % 2, :],
                                      in_=src1[:, j, :]), P.ldE)
                    if k == "mlp":
                        E.inc(E.dma_start(out=self.eb2[:, sub % 2, :],
                                          in_=P.dram["x1"][:, j, :]), P.ldE)
                E.wait(P.fin, self.b("fin") + (t + 1) * NSUB)
            E.inc(E.dma_start(out=self.out_d[:, t * NSUB:(t + 1) * NSUB, :],
                              in_=self.y[:]), P.st)
        nst_ = NTILE
        if self.has_gates and self.P.mode_io.get("gdump") == "out":
            E.inc(E.dma_start(out=P.dram["gdump"][:], in_=self.gs[:]), P.st)
            nst_ += 1
        if self.has_gates and self.P.mode_io.get("ldump") == "out":
            E.inc(E.dma_start(out=P.dram["ldump"][:], in_=self.ldmp[:]), P.st)
            nst_ += 1
        E.wait(P.st, self.b("st") + 16 * nst_)

    def e_tensor(self, E):
        P = self.P
        ones = P.sb["ones_bf"][:, 0:128]
        ident = P.sb["ident"][:, 0:128]
        pT = [P.ps["pT0"], P.ps["pT1"]]
        psA, psB, psC, psD = (P.ps["psA"], P.ps["psB"], P.ps["psC"], P.ps["psD"])
        pC = [psC[:, 0:512], psC[:, 512:1024]]
        for t in range(NTILE):
            E.wait(P.ld, self.ld_in(t))
            for sub in range(NSUB):
                kk = t * NSUB + sub
                if kk >= 2:
                    E.wait(P.te, self.b("te") + kk - 1)
                if self.kind == "mlp":
                    E.wait(P.aff, self.b("aff") + kk + 1)
                for c in range(NDC):
                    mm = E.op("transpose", pT[kk % 2][:, c * 128:(c + 1) * 128],
                              self.xt[:, sub, c * 128:(c + 1) * 128], ident)
                E.inc(mm, P.tp)
            E.wait(P.te, self.b("te") + (t + 1) * NSUB)
            if self.has_gates:
                E.wait(P.gwl, self.b("gwl") + 16 * self.wpre())
                for sub in range(NSUB):
                    kk = t * NSUB + sub
                    if kk >= 1:
                        E.wait(P.gc, self.b("gc") + kk)
                    for c in range(NDC):
                        mm = E.op("matmul", psD[:, 0:8],
                                  self.xf[:, c, sub * 128:(sub + 1) * 128],
                                  self.gw[:, c, 0:8],
                                  start=(c == 0), stop=(c == NDC - 1))
                    E.inc(mm, P.pg)
            for e in range(self.nexp):
                ge = t * self.nexp + e
                rhs_t = self.xe if self.kind == "l1" else self.xf
                if self.kind == "l1":
                    E.wait(P.aff, self.b("aff") + ge + 1)
                for q in range(NQ):
                    s = ge * NQ + q
                    slot = s % 2
                    E.wait(P.wlA, self.b("wlA") + 16 * (2 * s + 2))
                    g = 0
                    for fc in range(NFQ):
                        for th in range(NTH):
                            gi = s * G1 + g
                            bank = psA if gi % 2 == 0 else psB
                            if gi >= 2:
                                E.wait(P.ev1, self.b("ev1") + gi - 1)
                            for dc in range(NDC):
                                mm = E.op("matmul", bank[:],
                                          self.w1s[:, slot, dc, fc * 128:(fc + 1) * 128],
                                          rhs_t[:, dc, th * 512:(th + 1) * 512],
                                          start=(dc == 0), stop=(dc == NDC - 1))
                            E.inc(mm, P.pe1)
                            g += 1
                    E.wait(P.wlB, self.b("wlB") + 16 * (s + 1))
                    if q == 0:
                        nb2 = 1 if self.kind == "mlp" else ge + 1
                        E.wait(P.b2l, self.b("b2l") + 16 * nb2)
                    E.wait(P.ev1, self.b("ev1") + (s + 1) * G1)
                    g = 0
                    for tt in range(NSUB):
                        for dhh in range(2):
                            gi = s * G2 + g
                            bank = pC[gi % 2]
                            if gi >= 2:
                                E.wait(P.ev2, self.b("ev2") + gi - 1)
                            for fc in range(NFQ):
                                last = (fc == NFQ - 1) and (q != 0)
                                mm = E.op("matmul", bank,
                                          self.h[:, fc, tt * 128:(tt + 1) * 128],
                                          self.w2s[:, slot, fc, dhh * 512:(dhh + 1) * 512],
                                          start=(fc == 0), stop=last)
                            if q == 0:
                                mm = E.op("matmul", bank, ones,
                                          self.b2s[0:1, 0 if self.kind == "mlp" else ge % 2, dhh * 512:(dhh + 1) * 512],
                                          start=False, stop=True)
                            E.inc(mm, P.pe2)
                            g += 1

    def e_scalar(self, E):
        P = self.P
        k = self.kind
        for t in range(NTILE):
            if self.has_gates:
                for sub in range(NSUB):
                    kk = t * NSUB + sub
                    E.wait(P.ax, self.b("ax") + kk + 1)
                    E.op("activation", self.gt[:, 8:16], self.lgr[:],
                         AF.Exp, bias=self.gt[:, 2:3])
                    if k == "l1":
                        E.op("activation", self.gt[:, 4:5], self.gt[:, 1:2],
                             AF.Exp, bias=self.gt[:, 2:3])
                    E.incd(P.ge)
            for e in range(self.nexp):
                ge = t * self.nexp + e
                if k == "l1":
                    E.wait(P.te, self.b("te") + (t + 1) * NSUB)
                    E.wait(P.pe1, self.b("pe1") + ge * NQ * G1)
                    for c in range(NDC):
                        mm = E.op("activation", self.xe[:, c, :], self.xf[:, c, :],
                                  AF.Identity, scale=self.lngs[:, e, c:c + 1],
                                  bias=self.lnbs[:, e, c:c + 1])
                    E.inc(mm, P.aff)
                for q in range(NQ):
                    s = ge * NQ + q
                    for g in range(G1):
                        fc, th = g // NTH, g % NTH
                        gi = s * G1 + g
                        E.wait(P.pe1, self.b("pe1") + gi + 1)
                        bank = P.ps["psA"] if gi % 2 == 0 else P.ps["psB"]
                        E.inc(E.op("activation",
                                   self.h[:, fc, th * 512:(th + 1) * 512],
                                   bank[:], self.act,
                                   bias=self.b1s[:, s % 2, fc:fc + 1]), P.ev1)
            if k == "l1":
                for sub in range(NSUB):
                    j = t * NSUB + sub
                    E.wait(P.ldE, self.ld_epi(t, sub, 1))
                    E.inc(E.op("activation", self.et[:, sub % 2, :],
                               self.eb[:, sub % 2, :], AF.Identity,
                               scale=stat(P, ST_R1, j),
                               bias=stat(P, ST_NB1, j)), P.ep)
            elif k == "mlp":
                for sub in range(NSUB):
                    j = t * NSUB + sub
                    E.wait(P.ldE, self.ld_epi(t, sub, 2))
                    E.wait(P.tp, self.b("tp") + (t + 1) * NSUB)
                    E.wait(P.aff, self.b("aff") + j + 1)
                    E.inc(E.op("activation", self.et[:, sub % 2, :],
                               self.eb[:, sub % 2, :], AF.Exp,
                               scale=self.P.cf(-1.0)), P.ep)

    def e_vector(self, E):
        P = self.P
        k = self.kind
        pT = [P.ps["pT0"], P.ps["pT1"]]
        psC, psD = P.ps["psC"], P.ps["psD"]
        pC = [psC[:, 0:512], psC[:, 512:1024]]
        for t in range(NTILE):
            if t >= 1:
                E.wait(P.st, self.b("st") + 16 * t)
            if k == "mlp":
                E.wait(P.ld, self.ld_in(t))
                for sub in range(NSUB):
                    kk = t * NSUB + sub
                    if kk >= NSUB:
                        E.wait(P.tp, self.b("tp") + kk - NSUB + 1)
                    s1 = self.y[:, 0, :]
                    s2 = self.y[:, 1, :]
                    E.op("scalar_tensor_tensor", s1, self.g2b[:],
                         stat(P, ST_NBXA, kk), self.b2b[:],
                         op0=ALU.mult, op1=ALU.add)
                    E.op("scalar_tensor_tensor", s2, self.xa_t[:, sub, :],
                         stat(P, ST_RXA, kk), self.g2b[:],
                         op0=ALU.mult, op1=ALU.mult)
                    E.inc(E.op("tensor_add", self.xt[:, sub, :], s1, s2), P.aff)
            for sub in range(NSUB):
                kk = t * NSUB + sub
                E.wait(P.tp, self.b("tp") + kk + 1)
                E.inc(E.op("tensor_copy",
                           self.xf[:, :, sub * 128:(sub + 1) * 128],
                           pT[kk % 2].rearrange("p (c w) -> p c w", c=NDC)), P.te)
            if self.has_gates:
                gt = self.gt
                for sub in range(NSUB):
                    kk = t * NSUB + sub
                    E.wait(P.pg, self.b("pg") + kk + 1)
                    E.inc(E.op("tensor_copy", self.lgr[:], psD[:, 0:8]), P.gc)
                    if self.P.mode_io.get("ldump") == "out":
                        E.op("tensor_copy", self.ldmp[:, kk, :], self.lgr[:])
                    if k == "l1":
                        E.op("scalar_tensor_tensor", self.lgr[:], self.lgr[:],
                             stat(P, ST_SA, kk), self.rbbf[:, 0:8],
                             op0=ALU.mult, op1=ALU.add)
                    E.op("reduce_max", gt[:, 0:1], self.lgr[:], axis=AX.X)
                    E.op("tensor_scalar_mul", gt[:, 2:3], gt[:, 0:1], -1.0)
                    E.op("tensor_scalar", gt[:, 8:16], self.lgr[:],
                         gt[:, 0:1], -1e30, ALU.is_ge, ALU.mult)
                    E.op("tensor_add", gt[:, 8:16], gt[:, 8:16], self.lgr[:])
                    E.inc(E.op("reduce_max", gt[:, 1:2], gt[:, 8:16],
                               axis=AX.X), P.ax)
                    E.wait(P.ge, self.b("ge") + kk + 1)
                    if k == "l0":
                        E.op("reduce_sum", gt[:, 3:4], gt[:, 8:16], axis=AX.X)
                    else:
                        E.op("tensor_scalar_add", gt[:, 3:4], gt[:, 4:5], 1.0)
                    E.op("reciprocal", gt[:, 3:4], gt[:, 3:4])
                    E.op("tensor_scalar", gt[:, 8:16], gt[:, 8:16],
                         gt[:, 3:4], None, ALU.mult)
                    E.op("scalar_tensor_tensor", self.gs[:, kk, :],
                         self.lgr[:], gt[:, 1:2], gt[:, 8:16],
                         op0=ALU.is_ge, op1=ALU.mult)
            for e in range(self.nexp):
                for q in range(NQ):
                    s = (t * self.nexp + e) * NQ + q
                    for g in range(G2):
                        tt, dhh = g // 2, g % 2
                        gi = s * G2 + g
                        E.wait(P.pe2, self.b("pe2") + gi + 1)
                        ysl = self.y[:, tt, dhh * 512:(dhh + 1) * 512]
                        bank = pC[gi % 2]
                        if self.has_gates:
                            gAP = self.gs[:, t * NSUB + tt, e:e + 1]
                            if e == 0 and q == 0:
                                ins = E.op("tensor_scalar", ysl, bank, gAP,
                                           None, ALU.mult)
                            else:
                                ins = E.op("scalar_tensor_tensor", ysl, bank,
                                           gAP, ysl, op0=ALU.mult, op1=ALU.add)
                        else:
                            if q == 0:
                                ins = E.op("tensor_copy", ysl, bank)
                            else:
                                ins = E.op("tensor_add", ysl, ysl, bank)
                        E.inc(ins, P.ev2)
            if k == "l1":
                for sub in range(NSUB):
                    j = t * NSUB + sub
                    E.wait(P.ep, self.b("ep") + j + 1)
                    E.inc(E.op("tensor_add", self.y[:, sub, :],
                               self.y[:, sub, :], self.et[:, sub % 2, :]), P.fin)
            elif k == "mlp":
                for sub in range(NSUB):
                    j = t * NSUB + sub
                    E.wait(P.ep, self.b("ep") + j + 1)
                    eb = self.eb[:, sub % 2, :]
                    eb2 = self.eb2[:, sub % 2, :]
                    et = self.et[:, sub % 2, :]
                    E.op("tensor_scalar_add", et, et, 1.0)
                    E.op("reciprocal", et, et)
                    E.op("tensor_mul", et, eb, et)
                    E.op("tensor_add", eb, self.y[:, sub, :],
                         self.xa_t[:, sub, :])
                    E.op("tensor_mul", eb, eb, et)
                    E.inc(E.op("tensor_add", self.y[:, sub, :], eb, eb2), P.fin)


def lay_tm(x):
    return np.ascontiguousarray(x.reshape(NT, 128, -1).transpose(1, 0, 2))


def unlay_tm(x):
    return np.ascontiguousarray(x.transpose(1, 0, 2).reshape(N, -1))


def lay_fm(w):
    K = w.shape[1]
    return np.ascontiguousarray(w.reshape(NDC, 128, K).transpose(1, 0, 2))


def lay_expert_w1(w1):
    sh = w1.shape[:-2]
    r = w1.reshape(sh + (NDC, 128, NQ, FQ))
    order = tuple(range(len(sh))) + (len(sh) + 2, len(sh) + 1, len(sh), len(sh) + 3)
    return np.ascontiguousarray(r.transpose(order))


def lay_expert_b1(b1):
    sh = b1.shape[:-1]
    r = b1.reshape(sh + (NQ, NFQ, 128))
    order = tuple(range(len(sh))) + (len(sh), len(sh) + 2, len(sh) + 1)
    a = np.ascontiguousarray(r.transpose(order))
    out = np.zeros(sh + (NQ, 128, 128), np.float32)
    out[..., :NFQ] = a
    return out


def lay_expert_w2(w2):
    sh = w2.shape[:-2]
    r = w2.reshape(sh + (NQ, NFQ, 128, D))
    order = tuple(range(len(sh))) + (len(sh), len(sh) + 2, len(sh) + 1, len(sh) + 3)
    return np.ascontiguousarray(r.transpose(order))


def host_inputs(ins):
    g = {}
    g["wq"] = lay_fm(ins["L0_attn_Wq"])
    g["wk"] = lay_fm(ins["L0_attn_Wk"])
    g["wv"] = lay_fm(ins["L0_attn_Wv"])
    g["wq2"] = lay_fm(ins["L0_attn_Wq2"])
    g["wk2"] = lay_fm(ins["L0_attn_Wk2"])
    g["wv2"] = lay_fm(ins["L0_attn_Wv2"])
    g["wo"] = lay_fm(ins["L0_attn_Wo"])
    bq = ins["L0_attn_bq"].reshape(NDC, 128).T
    bk = ins["L0_attn_bk"].reshape(NDC, 128).T
    bv = ins["L0_attn_bv"].reshape(NDC, 128).T
    g["bqkv1"] = np.zeros((128, 3, 64), np.float32)
    g["bqkv1"][:, :, 0:NDC] = np.stack([bq, bk, bv], axis=1)
    rows0 = np.zeros((1, 4, D), np.float32)
    rows0[0, 0] = ins["L0_attn_bq2"]
    rows0[0, 1] = ins["L0_attn_bk2"]
    rows0[0, 2] = ins["L0_attn_bv2"]
    rows0[0, 3] = ins["L0_attn_bo"]
    g["rows0"] = rows0
    g["g0rep"] = np.broadcast_to(ins["L0_attn_gamma"][None, :], (128, D)).copy()
    g["be0rep"] = np.broadcast_to(ins["L0_attn_beta"][None, :], (128, D)).copy()
    gw0 = lay_fm(np.ascontiguousarray(ins["L0_ffn_gate_w"].T))
    g["gw0"] = np.zeros((128, NDC, 32), np.float32)
    g["gw0"][:, :, 0:8] = gw0
    g["w1e0"] = lay_expert_w1(ins["L0_ffn_w1"])
    g["b1e0"] = lay_expert_b1(ins["L0_ffn_b1"])
    g["w2e0"] = lay_expert_w2(ins["L0_ffn_w2"])
    g["b2e0"] = ins["L0_ffn_b2"]
    g["wqkv"] = lay_fm(ins["L1_attn_w_qkv"])
    g["l1grep"] = np.broadcast_to(ins["L1_attn_ln1_g"][None, :], (128, D)).copy()
    g["l1brep"] = np.broadcast_to(ins["L1_attn_ln1_b"][None, :], (128, D)).copy()
    g["bqkv"] = np.zeros((128, 128), np.float32)
    g["bqkv"][:, 0:24] = ins["L1_attn_b_qkv"].reshape(24, 128).T
    g["wout1"] = lay_fm(ins["L1_attn_w_out"])
    rows1 = np.zeros((1, 8, D), np.float32)
    rows1[0, 0] = ins["L1_attn_b_out"]
    rows1[0, 1] = ins["L1_attn_ln1_g"]
    rows1[0, 2] = ins["L1_attn_ln1_b"]
    rows1[0, 3] = ins["L1_attn_ln2_g"]
    rows1[0, 4] = ins["L1_attn_ln2_b"]
    rows1[0, 5] = ins["L1_attn_b2"]
    g["rows1"] = rows1
    g["g2rep"] = np.broadcast_to(ins["L1_attn_ln2_g"][None, :], (128, D)).copy()
    g["b2rep"] = np.broadcast_to(ins["L1_attn_ln2_b"][None, :], (128, D)).copy()
    g["b2mrow"] = ins["L1_attn_b2"].reshape(1, D)
    g["w1m"] = lay_expert_w1(ins["L1_attn_w1"])
    g["b1m"] = lay_expert_b1(ins["L1_attn_b1"])
    g["w2m"] = lay_expert_w2(ins["L1_attn_w2"])
    gw1 = lay_fm(ins["L1_ffn_router_W"])
    g["gw1"] = np.zeros((128, NDC, 32), np.float32)
    g["gw1"][:, :, 0:8] = gw1
    rb = np.zeros((128, 128), np.float32)
    rb[:, 0:8] = ins["L1_ffn_router_b"][None, :]
    g["rb1"] = rb
    lng = ins["L1_ffn_ln_g"].reshape(NEXP, NDC, 128).transpose(2, 0, 1)
    g["lng"] = np.zeros((128, NEXP, 16), np.float32)
    g["lng"][:, :, 0:NDC] = lng
    lnb = ins["L1_ffn_ln_b"].reshape(NEXP, NDC, 128).transpose(2, 0, 1)
    g["lnb"] = np.zeros((128, NEXP, 16), np.float32)
    g["lnb"][:, :, 0:NDC] = lnb
    g["w1e1"] = lay_expert_w1(ins["L1_ffn_W1"])
    g["b1e1"] = lay_expert_b1(ins["L1_ffn_b1"])
    g["w2e1"] = lay_expert_w2(ins["L1_ffn_W2"])
    g["b2e1"] = ins["L1_ffn_b2"]
    return {k: np.ascontiguousarray(v, dtype=np.float32) for k, v in g.items()}


NG2 = 8
GS = 512
WQ, WK, WV, WQ2, WK2, WV2, WO = range(7)


class PH2(Phase):
    name = "ph2"
    TP_G = 8; TE_G = 8; PE1_G = 24; EV1_G = 28; PE2_G = 24; EV2_G = 16
    AX_G = 16; GE_G = 16; GV_G = 4; PG_G = 4
    BPE1_G = 8; BEV1_G = 8; BPE2_G = 8; BEV2_G = 8
    BAX_S = 3; BGE_S = 4; BFIN_S = 2

    def setup(self):
        P = self.P
        self.w7 = P.sbuf("ph2_w7", SB_XF, [128, 7, NDC, D], BF16)
        self.axt = P.sbuf("ph2_axt", SB_WB, [128, 4, D], BF16)
        self.axf = P.sbuf("ph2_axf", SB_WB + 8192, [128, NDC, GS], BF16)
        self.p1f = P.sbuf("ph2_p1f", SB_WB + 16384, [128, NDC, GS], BF16)
        self.vsc = P.sbuf("ph2_vsc", SB_WB + 24576, [128, 4, D], BF16)
        self.qe = P.sbuf("ph2_qe", SB_FREE, [128, 4, D], BF16)
        self.ke = P.sbuf("ph2_ke", SB_FREE + 8192, [128, 4, D], BF16)
        self.y0g = P.sbuf("ph2_y0g", SB_FREE, [128, 4, D], F32)
        self.vf = P.sbuf("ph2_vf", SB_FREE + 16384, [128, NDC, GS], BF16)
        self.qk = P.sbuf("ph2_qk", SB_FREE + 24576, [128, D], F32)
        self.eluA = P.sbuf("ph2_eluA", SB_FREE + 28672, [128, GS], F32)
        self.eluB = P.sbuf("ph2_eluB", SB_FREE + 30720, [128, GS], F32)
        self.scrA = P.sbuf("ph2_scrA", SB_FREE + 28672, [128, 2, D], F32)
        self.rowsA = P.sbuf("ph2_rowsA", SB_FREE + 36864, [1, 3, D], BF16)
        self.scrB = P.sbuf("ph2_scrB", SB_FREE + 36864, [128, 2, D], F32)
        self.xhst = P.sbuf("ph2_xhst", SB_FREE + 45056, [128, 2, D], BF16)
        self.g0b = P.sbuf("ph2_g0b", SB_FREE + 49152, [128, D], BF16)
        self.be0b = P.sbuf("ph2_be0b", SB_FREE + 51200, [128, D], BF16)
        self.borow = P.sbuf("ph2_borow", SB_FREE + 53248, [1, 1, D], BF16)
        self.qksr = P.sbuf("ph2_qksr", SB_FREE + 55296, [128, 4, 16], F32)
        self.bq1 = P.sbuf("ph2_bq1", SB_FREE + 55552, [128, 3, 64], F32)
        self.st16 = P.sbuf("ph2_st16", SB_FREE + 56320, [128, 16], F32)
        self.S32 = P.sbuf("ph2_S32", SB_WB, [128, 512], F32)
        self.Sbf = P.sbuf("ph2_Sbf", SB_WB + 2048, [128, 512], BF16)
        self.af = P.sbuf("ph2_af", SB_WB + 4096, [128, NDC, GS], BF16)
        self.bxt = P.sbuf("ph2_bxt", SB_WB + 12288, [128, 4, D], F32)

    def stg(self):
        return 1 if self.P.mode_io.get("adump") == "out" else 0

    def st_x1(self, n):
        return self.b("st") + 16 * (NG2 + (n // 4 + 1) * self.stg() + 2 * n + 1)

    def st_xh(self, n):
        return self.b("st") + 16 * (NG2 + (n // 4 + 1) * self.stg() + 2 * n + 2)

    def e_gpsimd(self, E):
        P = self.P
        for i, w in enumerate(("wq", "wk", "wv", "wq2", "wk2", "wv2", "wo")):
            E.inc(E.dma_start(out=self.w7[:, i], in_=P.dram[w]), P.wl)
        E.inc(E.dma_start(out=self.rowsA[:], in_=P.dram["rows0"][:, 0:3, :]), P.wl)
        E.inc(E.dma_start(out=self.borow[:], in_=P.dram["rows0"][:, 3:4, :]), P.wl)
        E.inc(E.dma_start(out=self.g0b[:], in_=P.dram["g0rep"]), P.wl)
        E.inc(E.dma_start(out=self.be0b[:], in_=P.dram["be0rep"]), P.wl)

    def e_sync(self, E):
        P = self.P
        E.inc(E.dma_start(out=self.bq1[:], in_=P.dram["bqkv1"]), P.ld)

        for g in range(NG2):
            if g >= 1:
                E.wait(P.tp, self.b("tp") + g * self.TP_G - 4)
            E.inc(E.dma_start(out=self.axt[:],
                              in_=P.dram["xn0"][:, 4 * g:4 * g + 4, :]), P.ld)
            E.wait(P.te, self.b("te") + g * self.TE_G + 8)
            E.inc(E.dma_start(out=P.dram["ved"][:, g], in_=self.vf[:]), P.st)
        for g in range(NG2):
            if g == 0:
                E.wait(P.st, self.b("st") + 16 * NG2)
            else:
                E.wait(P.pe1, self.b("pe1") + NG2 * self.PE1_G + g * self.BPE1_G)
            E.inc(E.dma_start(out=self.vf[:], in_=P.dram["ved"][:, g]), P.ld)
            E.inc(E.dma_start(out=self.bxt[:],
                              in_=P.dram["x_in"][:, 4 * g:4 * g + 4, :]), P.ld)
            E.inc(E.dma_start(out=self.y0g[:],
                              in_=P.dram["y0"][:, 4 * g:4 * g + 4, :]), P.ld)
            if self.P.mode_io.get("adump") == "out":
                E.wait(P.ev1, self.b("ev1") + NG2 * self.EV1_G
                       + g * self.BEV1_G + NDC)
                E.inc(E.dma_start(out=P.dram["adump"][:, g], in_=self.af[:]), P.st)
            for sub in range(4):
                n = 4 * g + sub
                E.wait(P.fin, self.b("fin") + 2 * n + 1)
                E.inc(E.dma_start(out=P.dram["x1"][:, n, :],
                                  in_=self.scrA[:, sub % 2, :]), P.st)
                E.wait(P.fin, self.b("fin") + 2 * n + 2)
                E.inc(E.dma_start(out=P.dram["xh1"][:, n, :],
                                  in_=self.xhst[:, sub % 2, :]), P.st)
        E.inc(E.dma_start(out=P.dram["stats"][:],
                          in_=P.sb["stat"][:]), P.st)
        nst_ = NG2 + 2 * NT + 1 + NG2 * self.stg()
        if self.P.mode_io.get("sdump") == "out":
            E.wait(P.gc, self.b("gc") + 2)
            E.inc(E.dma_start(out=P.dram["sdump"][:], in_=self.S32[:]), P.st)
            nst_ += 1
        E.wait(P.st, self.b("st") + 16 * nst_)

    def pc_release(self, u_glob):
        g, r = divmod(u_glob, self.PE2_G)
        if r < 16:
            return ("ev2", self.b("ev2") + g * self.EV2_G + r + 1)
        return ("ev1", self.b("ev1") + g * self.EV1_G + 24 + (r - 16) // 2 + 1)

    def e_tensor(self, E):
        P = self.P
        ones = P.sb["ones_bf"][:, 0:128]
        ident = P.sb["ident"][:, 0:128]
        pT = [P.ps["pT0"], P.ps["pT1"]]
        psA, psB, psC, psD = (P.ps["psA"], P.ps["psB"], P.ps["psC"], P.ps["psD"])
        pC = [psC[:, 0:512], psC[:, 512:1024]]
        E.wait(P.ld, self.b("ld") + 16 * 1)
        E.wait(P.wl, self.b("wl") + 16 * 11)
        first_S = [True]
        for g in range(NG2):
            tpb = self.b("tp") + g * self.TP_G
            teb = self.b("te") + g * self.TE_G
            pe1b = self.b("pe1") + g * self.PE1_G
            ev1b = self.b("ev1") + g * self.EV1_G
            pe2b = self.b("pe2") + g * self.PE2_G
            ev2b = self.b("ev2") + g * self.EV2_G
            E.wait(P.ld, self.b("ld") + 16 * (1 + g + 1))
            for sub in range(4):
                ks = g * self.TP_G + sub
                if ks >= 2:
                    E.wait(P.te, self.b("te") + ks - 1)
                for c in range(NDC):
                    mm = E.op("transpose", pT[ks % 2][:, c * 128:(c + 1) * 128],
                              self.axt[:, sub, c * 128:(c + 1) * 128], ident)
                E.inc(mm, P.tp)
            for pi, (wi, w2i) in enumerate(((WQ, WQ2), (WK, WK2), (WV, WV2))):
                E.wait(P.te, teb + 4)
                for j in range(NDC):
                    u = g * self.PE1_G + pi * 8 + j
                    bank = psA if u % 2 == 0 else psB
                    if u >= 2:
                        gg, rr = divmod(u - 2, self.PE1_G)
                        E.wait(P.ev1, self.b("ev1") + gg * self.EV1_G + rr + 1)
                    for c in range(NDC):
                        mm = E.op("matmul", bank[:],
                                  self.w7[:, wi, c, j * 128:(j + 1) * 128],
                                  self.axf[:, c, :],
                                  start=(c == 0), stop=(c == NDC - 1))
                    E.inc(mm, P.pe1)
                for sub in range(4):
                    for hf in range(2):
                        u2 = g * self.PE2_G + pi * 8 + sub * 2 + hf
                        bank = pC[u2 % 2]
                        if u2 >= 2:
                            sem, val = self.pc_release(u2 - 2)
                            E.wait(P.sc[sem] if False else getattr(P, sem), val)
                        E.wait(P.ev1, ev1b + pi * 8 + NDC)
                        for c in range(NDC):
                            mm = E.op("matmul", bank,
                                      self.p1f[:, c, sub * 128:(sub + 1) * 128],
                                      self.w7[:, w2i, c, hf * 512:(hf + 1) * 512],
                                      start=(c == 0), stop=False)
                        mm = E.op("matmul", bank, ones,
                                  self.rowsA[0:1, pi, hf * 512:(hf + 1) * 512],
                                  start=False, stop=True)
                        E.inc(mm, P.pe2)
            for sub in range(4):
                E.wait(P.ev2, ev2b + 8 + 2 * sub + 2)
                for h in range(NHEAD):
                    mm = E.op("matmul",
                              psD[(h % 2) * 64:(h % 2) * 64 + 64,
                                  (h // 2) * 64:(h // 2) * 64 + 64],
                              self.ke[:, sub, h * 64:(h + 1) * 64],
                              self.qe[:, sub, h * 64:(h + 1) * 64],
                              start=first_S[0],
                              stop=(g == NG2 - 1 and sub == 3 and h == NHEAD - 1),
                              skip_group_check=True,
                              tile_position=(0, (h % 2) * 64))
                    first_S[0] = False
                E.inc(mm, P.pg)
            for sub in range(4):
                ks = g * self.TP_G + 4 + sub
                E.wait(P.ev1, ev1b + 24 + sub + 1)
                if ks >= 2:
                    E.wait(P.te, self.b("te") + ks - 1)
                for c in range(NDC):
                    mm = E.op("transpose", pT[ks % 2][:, c * 128:(c + 1) * 128],
                              self.vsc[:, sub, c * 128:(c + 1) * 128], ident)
                E.inc(mm, P.tp)
        ld0 = self.b("ld") + 16 * (1 + NG2)
        pe1B = self.b("pe1") + NG2 * self.PE1_G
        ev1B = self.b("ev1") + NG2 * self.EV1_G
        pe2B = self.b("pe2") + NG2 * self.PE2_G
        ev2B = self.b("ev2") + NG2 * self.EV2_G
        E.wait(P.gc, self.b("gc") + 2)
        for g in range(NG2):
            E.wait(P.ld, ld0 + 16 * (3 * g + 3))
            for cc in range(NDC):
                u = g * self.BPE1_G + cc
                bank = psA if u % 2 == 0 else psB
                if u >= 2:
                    E.wait(P.ev1, ev1B + u - 1)
                for hh in range(2):
                    h = 2 * cc + hh
                    mm = E.op("matmul",
                              bank[(h % 2) * 64:(h % 2) * 64 + 64, :],
                              self.Sbf[(h % 2) * 64:(h % 2) * 64 + 64,
                                       (h // 2) * 64:(h // 2) * 64 + 64],
                              self.vf[(h % 2) * 64:(h % 2) * 64 + 64, h // 2, :],
                              start=True, stop=True,
                              tile_position=((h % 2) * 64, (h % 2) * 64))
                E.inc(mm, P.pe1)
            for sub in range(4):
                for hf in range(2):
                    u2 = g * self.BPE2_G + sub * 2 + hf
                    bank = pC[u2 % 2]
                    if u2 < 2:
                        E.wait(P.ev1, ev1B)
                    else:
                        E.wait(P.ev2, ev2B + u2 - 1)
                    E.wait(P.ev1, ev1B + g * self.BEV1_G + NDC)
                    for c in range(NDC):
                        mm = E.op("matmul", bank,
                                  self.af[:, c, sub * 128:(sub + 1) * 128],
                                  self.w7[:, WO, c, hf * 512:(hf + 1) * 512],
                                  start=(c == 0), stop=False)
                    mm = E.op("matmul", bank, ones,
                              self.borow[0:1, 0, hf * 512:(hf + 1) * 512],
                              start=False, stop=True)
                    E.inc(mm, P.pe2)

    def e_scalar(self, E):
        P = self.P
        for g in range(NG2):
            pe1b = self.b("pe1") + g * self.PE1_G
            pe2b = self.b("pe2") + g * self.PE2_G
            for pi in range(3):
                for j in range(NDC):
                    u = pi * 8 + j
                    E.wait(P.pe1, pe1b + u + 1)
                    bank = P.ps["psA"] if (g * self.PE1_G + u) % 2 == 0 else P.ps["psB"]
                    E.inc(E.op("activation", self.p1f[:, j, :], bank[:],
                               AF.Identity, scale=1.0,
                               bias=self.bq1[:, pi, j:j + 1]), P.ev1)
                if pi < 2:
                    for sub in range(4):
                        for hf in range(2):
                            kk = g * self.AX_G + pi * 8 + sub * 2 + hf
                            E.wait(P.ax, self.b("ax") + kk + 1)
                            E.op("activation", self.eluA[:], self.eluA[:], AF.Exp)
                            E.incd(P.ge)
            for sub in range(4):
                E.wait(P.gv, self.b("gv") + g * self.GV_G + sub + 1)
                E.wait(P.pe2, pe2b + 16 + sub * 2 + 2)
                if g >= 1:
                    E.wait(P.tp, self.b("tp") + g * self.TP_G)
                for h in range(NHEAD):
                    hf = (h * 64) // 512
                    bi = (16 + sub * 2 + hf) % 2
                    bank = P.ps["psC"][:, 0:512] if bi == 0 else P.ps["psC"][:, 512:1024]
                    mm = E.op("activation",
                              self.vsc[:, sub, h * 64:(h + 1) * 64],
                              bank[:, (h * 64) % 512:(h * 64) % 512 + 64],
                              AF.Identity, scale=self.qksr[:, sub, h:h + 1])
                E.inc(mm, P.ev1)
        axB = self.b("ax") + NG2 * self.AX_G
        geB = self.b("ge") + NG2 * self.GE_G
        ld0 = self.b("ld") + 16 * (1 + NG2)
        for g in range(NG2):
            for sub in range(4):
                n = 4 * g + sub
                E.wait(P.ax, axB + 3 * n + 1)
                E.op("activation", self.st16[:, 1:2], self.st16[:, 0:1],
                     AF.Sqrt, bias=P.cf(EPSG))
                E.incd(P.ge)
                E.wait(P.ld, ld0 + 16 * (3 * g + 3))
                if n >= 2:
                    E.wait(P.st, self.st_x1(n) - 16)
                E.op("activation", self.scrA[:, sub % 2, :],
                     self.y0g[:, sub, :], AF.Exp, scale=P.cf(-1.0))
                E.incd(P.ge)
                E.wait(P.ax, axB + 3 * n + 2)
                E.op("activation", self.st16[:, 6:7], self.st16[:, 5:6],
                     AF.Sqrt, bias=P.cf(EPS0))
                E.incd(P.ge)
                E.wait(P.ax, axB + 3 * n + 3)
                E.op("activation", self.st16[:, 9:10], self.st16[:, 8:9],
                     AF.Sqrt, bias=P.cf(EPS0))
                E.op("activation", stat(P, ST_SA, n), self.st16[:, 9:10],
                     AF.Identity)
                E.incd(P.ge)

    def e_vector(self, E):
        P = self.P
        pT = [P.ps["pT0"], P.ps["pT1"]]
        psC, psD = P.ps["psC"], P.ps["psD"]
        pC = [psC[:, 0:512], psC[:, 512:1024]]
        for g in range(NG2):
            tpb = self.b("tp") + g * self.TP_G
            pe2b = self.b("pe2") + g * self.PE2_G
            if g >= 1:
                E.wait(P.pg, self.b("pg") + g * self.PG_G)
                E.wait(P.ev1, self.b("ev1") + g * self.EV1_G)
                E.wait(P.st, self.b("st") + 16 * g)
            for sub in range(4):
                ks = g * self.TP_G + sub
                E.wait(P.tp, self.b("tp") + ks + 1)
                E.inc(E.op("tensor_copy",
                           self.axf[:, :, sub * 128:(sub + 1) * 128],
                           pT[ks % 2].rearrange("p (c w) -> p c w", c=NDC)), P.te)
            for pi in range(2):
                dst = self.qe if pi == 0 else self.ke
                for sub in range(4):
                    for hf in range(2):
                        u2 = pi * 8 + sub * 2 + hf
                        kk = g * self.AX_G + u2
                        E.wait(P.pe2, pe2b + u2 + 1)
                        bank = pC[(g * self.PE2_G + u2) % 2]
                        E.op("tensor_scalar_min", self.eluA[:], bank, 0.0)
                        E.inc(E.op("tensor_scalar_max", self.eluB[:], bank, 0.0),
                              P.ax)
                        E.wait(P.ge, self.b("ge") + kk + 1)
                        E.inc(E.op("tensor_add",
                                   dst[:, sub, hf * 512:(hf + 1) * 512],
                                   self.eluA[:], self.eluB[:]), P.ev2)
            for sub in range(4):
                E.op("tensor_mul", self.qk[:], self.qe[:, sub, :], self.ke[:, sub, :])
                E.op("reduce_sum", self.qksr[:, sub, :],
                     self.qk.rearrange("p (h w) -> p h w", h=16), axis=AX.X)
                E.op("tensor_scalar_add", self.qksr[:, sub, :],
                     self.qksr[:, sub, :], EPSQK)
                E.inc(E.op("reciprocal", self.qksr[:, sub, :],
                           self.qksr[:, sub, :]), P.gv)
            for sub in range(4):
                ks = g * self.TP_G + 4 + sub
                E.wait(P.tp, self.b("tp") + ks + 1)
                E.inc(E.op("tensor_copy",
                           self.vf[:, :, sub * 128:(sub + 1) * 128],
                           pT[ks % 2].rearrange("p (c w) -> p c w", c=NDC)), P.te)
        E.wait(P.pg, self.b("pg") + NG2 * self.PG_G)
        E.inc(E.op("tensor_copy", self.S32[:], psD[:, 0:512]), P.gc)
        E.inc(E.op("tensor_copy", self.Sbf[:], self.S32[:]), P.gc)
        axB = self.b("ax") + NG2 * self.AX_G
        geB = self.b("ge") + NG2 * self.GE_G
        ev1B = self.b("ev1") + NG2 * self.EV1_G
        pe1B = self.b("pe1") + NG2 * self.PE1_G
        pe2B = self.b("pe2") + NG2 * self.PE2_G
        ev2B = self.b("ev2") + NG2 * self.EV2_G
        st16 = self.st16
        for g in range(NG2):
            for cc in range(NDC):
                u = g * self.BPE1_G + cc
                E.wait(P.pe1, pe1B + u + 1)
                bank = P.ps["psA"] if u % 2 == 0 else P.ps["psB"]
                E.inc(E.op("tensor_copy", self.af[:, cc, :], bank[:]), P.ev1)
            for sub in range(4):
                n = 4 * g + sub
                p = sub % 2
                yt = self.qk
                for hf in range(2):
                    u2 = g * self.BPE2_G + sub * 2 + hf
                    E.wait(P.pe2, pe2B + u2 + 1)
                    bank = pC[u2 % 2]
                    E.op("scalar_tensor_tensor",
                         yt[:, hf * 512:(hf + 1) * 512],
                         self.bxt[:, sub, hf * 512:(hf + 1) * 512],
                         stat(P, ST_R0, n), bank,
                         op0=ALU.mult, op1=ALU.add)
                    E.inc(E.op("tensor_scalar",
                               yt[:, hf * 512:(hf + 1) * 512],
                               yt[:, hf * 512:(hf + 1) * 512],
                               stat(P, ST_NB0, n), None, ALU.add), P.ev2)
                E.op("reduce_sum", st16[:, 12:13], yt[:], axis=AX.X)
                if n >= 2:
                    E.wait(P.st, self.st_x1(n) - 16)
                E.op("tensor_mul", self.scrB[:, p, :], yt[:], yt[:])
                E.op("reduce_sum", st16[:, 13:14], self.scrB[:, p, :], axis=AX.X)
                E.op("tensor_scalar_mul", st16[:, 14:15], st16[:, 12:13], 1.0 / D)
                E.op("tensor_scalar_mul", st16[:, 13:14], st16[:, 13:14], 1.0 / D)
                E.op("tensor_mul", st16[:, 15:16], st16[:, 14:15], st16[:, 14:15])
                E.inc(E.op("tensor_sub", st16[:, 0:1], st16[:, 13:14],
                           st16[:, 15:16]), P.ax)
                E.wait(P.ge, geB + 4 * n + 1)
                E.op("reciprocal", st16[:, 2:3], st16[:, 1:2])
                E.op("tensor_scalar", st16[:, 3:4], st16[:, 14:15],
                     st16[:, 2:3], -1.0, ALU.mult, ALU.mult)
                E.op("tensor_scalar", yt[:], yt[:], st16[:, 2:3], st16[:, 3:4],
                     ALU.mult, ALU.add)
                E.op("tensor_mul", yt[:], yt[:], self.g0b[:])
                E.op("tensor_add", yt[:], yt[:], self.be0b[:])
                E.wait(P.ge, geB + 4 * n + 2)
                E.op("tensor_scalar_add", self.scrA[:, p, :], self.scrA[:, p, :], 1.0)
                E.op("reciprocal", self.scrA[:, p, :], self.scrA[:, p, :])
                E.op("tensor_mul", self.scrA[:, p, :], self.scrA[:, p, :],
                     self.y0g[:, sub, :])
                E.op("tensor_mul", yt[:], yt[:], self.scrA[:, p, :])
                E.inc(E.op("tensor_add", self.scrA[:, p, :], yt[:],
                           self.bxt[:, sub, :]), P.fin)
                E.op("reduce_sum", st16[:, 4:5], self.scrA[:, p, :], axis=AX.X)
                E.op("tensor_mul", self.scrB[:, p, :], self.scrA[:, p, :],
                     self.scrA[:, p, :])
                E.op("reduce_sum", st16[:, 5:6], self.scrB[:, p, :], axis=AX.X)
                E.op("tensor_scalar_mul", st16[:, 15:16], st16[:, 4:5], 1.0 / D)
                E.op("tensor_scalar_mul", st16[:, 5:6], st16[:, 5:6], 1.0 / D)
                E.op("tensor_mul", st16[:, 4:5], st16[:, 15:16], st16[:, 15:16])
                E.inc(E.op("tensor_sub", st16[:, 5:6], st16[:, 5:6],
                           st16[:, 4:5]), P.ax)
                E.wait(P.ge, geB + 4 * n + 3)
                E.op("reciprocal", stat(P, ST_R1, n), st16[:, 6:7])
                E.op("tensor_scalar", stat(P, ST_NB1, n), st16[:, 15:16],
                     stat(P, ST_R1, n), -1.0, ALU.mult, ALU.mult)
                E.op("tensor_scalar", self.scrB[:, p, :], self.scrA[:, p, :],
                     stat(P, ST_R1, n), stat(P, ST_NB1, n), ALU.mult, ALU.add)
                E.op("reduce_sum", st16[:, 7:8], self.scrB[:, p, :], axis=AX.X)
                E.op("tensor_mul", yt[:], self.scrB[:, p, :], self.scrB[:, p, :])
                E.op("reduce_sum", st16[:, 8:9], yt[:], axis=AX.X)
                E.op("tensor_scalar_mul", st16[:, 10:11], st16[:, 7:8], 1.0 / D)
                E.op("tensor_scalar_mul", st16[:, 8:9], st16[:, 8:9], 1.0 / D)
                E.op("tensor_mul", st16[:, 7:8], st16[:, 10:11], st16[:, 10:11])
                E.inc(E.op("tensor_sub", st16[:, 8:9], st16[:, 8:9],
                           st16[:, 7:8]), P.ax)
                E.wait(P.ge, geB + 4 * n + 4)
                E.op("reciprocal", stat(P, ST_RA, n), st16[:, 9:10])
                E.op("tensor_scalar", stat(P, ST_NBA, n), st16[:, 10:11],
                     stat(P, ST_RA, n), -1.0, ALU.mult, ALU.mult)
                E.inc(E.op("tensor_scalar", self.xhst[:, p, :],
                           self.scrB[:, p, :], stat(P, ST_RA, n),
                           stat(P, ST_NBA, n), ALU.mult, ALU.add), P.fin)


class PH4(Phase):
    name = "ph4"
    TP_G = 20; TE_G = 20; PE1_G = 32; EV1_G = 24
    PE2_G = 8; PG_G = 8; GC_G = 8; EV2_G = 12
    AX_G = 12; GE_G = 12; EP_G = 8; AFF_G = 4; GV_G = 1; FIN_G = 4
    NG = NG2
    DBG = frozenset()

    def setup(self):
        P = self.P
        self.wqkv = P.sbuf("ph4_wqkv", SB_XF, [128, NDC, 3 * D], BF16)
        self.wout = P.sbuf("ph4_wout", SB_Y, [128, NDC, D], BF16)
        self.l1gb = P.sbuf("ph4_l1gb", SB_Y + 16384, [128, D], BF16)
        self.l1bb = P.sbuf("ph4_l1bb", SB_Y + 18432, [128, D], BF16)
        self.borow = P.sbuf("ph4_borow", SB_Y + 20480, [1, 1, D], BF16)
        self.qkf = P.sbuf("ph4_qkf", SB_WA, [128, 24, GS], BF16)
        self.scf = P.sbuf("ph4_scf", SB_WA + 24576, [128, 8, 128], F32)
        self.atb = P.sbuf("ph4_atb", SB_WA + 28672, [128, 8, 128], BF16)
        self.atT = P.sbuf("ph4_atT", SB_WA + 30720, [128, 8, 128], BF16)
        self.xht = P.sbuf("ph4_xht", SB_WB, [128, 4, D], BF16)
        self.h2t = P.sbuf("ph4_h2t", SB_WB + 8192, [128, 4, D], BF16)
        self.hf = P.sbuf("ph4_hf", SB_WB + 16384, [128, NDC, GS], BF16)
        self.vt = P.sbuf("ph4_vt", SB_WB + 24576, [128, 4, D], BF16)
        self.obt = P.sbuf("ph4_obt", SB_FREE, [128, 4, D], BF16)
        self.obf = P.sbuf("ph4_obf", SB_FREE + 8192, [128, NDC, GS], BF16)
        self.xa_t = P.sbuf("ph4_xat", SB_FREE + 16384, [128, 2, D], F32)
        self.bxt = P.sbuf("ph4_bxt", SB_FREE + 24576, [128, 4, D], F32)
        self.mask = P.sbuf("ph4_mask", SB_FREE + 40960, [128, 8, 128], BF16)
        self.mask2 = P.sbuf("ph4_mask2", SB_FREE + 43008, [128, 8, 128], BF16)
        self.ktail = P.sbuf("ph4_ktail", SB_FREE + 45056, [128, 8, 64], BF16)
        self.vtail = P.sbuf("ph4_vtail", SB_FREE + 46080, [128, D], BF16)
        self.bqs = P.sbuf("ph4_bqs", SB_FREE + 48128, [128, 128], F32)
        self.bq8 = P.sbuf("ph4_bq8", SB_FREE + 48640, [128, 8], F32)
        self.nm = P.sbuf("ph4_nm", SB_FREE + 48704, [128, 8], F32)
        self.rr = P.sbuf("ph4_rr", SB_FREE + 48768, [128, 8], F32)
        self.stb = P.sbuf("ph4_stb", SB_FREE + 48832, [128, 16], F32)
        self.yt2 = P.sbuf("ph4_yt2", SB_FREE + 49408, [128, D], F32)
        self.vw = P.sbuf("ph4_vw", SB_FREE + 53504, [128, D], BF16)
        mo = np.zeros((128, 8, 128), np.float32)
        me = np.zeros((128, 8, 128), np.float32)
        for p in range(128):
            qi = p % 64
            for j in range(128):
                if j >= 64 and (j - 64) > qi:
                    mo[p, :, j] = -1e30
                if j < 64 and j > qi:
                    me[p, :, j] = -1e30
        self.c_mask = P.const("c_mask_ph4o", mo.astype(BF))
        self.c_mask2 = P.const("c_mask_ph4e", me.astype(BF))

    def e_gpsimd(self, E):
        P = self.P
        E.inc(E.dma_start(out=self.wqkv[:], in_=P.dram["wqkv"]), P.wl)
        E.inc(E.dma_start(out=self.wout[:], in_=P.dram["wout1"]), P.wl)
        E.inc(E.dma_start(out=self.l1gb[:], in_=P.dram["l1grep"]), P.wl)
        E.inc(E.dma_start(out=self.l1bb[:], in_=P.dram["l1brep"]), P.wl)
        E.inc(E.dma_start(out=self.borow[:], in_=P.dram["rows1"][:, 0:1, :]), P.wl)

    def e_sync(self, E):
        P = self.P
        E.inc(E.dma_start(out=self.bqs[:], in_=P.dram["bqkv"]), P.ld)
        E.inc(E.dma_start(out=self.mask[:], in_=self.c_mask), P.ld)
        E.inc(E.dma_start(out=self.mask2[:], in_=self.c_mask2), P.ld)
        for g in range(self.NG):
            if g >= 1:
                E.wait(P.tp, self.b("tp") + g * self.TP_G - 16)
            E.inc(E.dma_start(out=self.xht[:],
                              in_=P.dram["xh1"][:, 4 * g:4 * g + 4, :]), P.ld)
            if g >= 1:
                E.wait(P.ev2, self.b("ev2") + g * self.EV2_G)
            E.inc(E.dma_start(out=self.bxt[:],
                              in_=P.dram["x1"][:, 4 * g:4 * g + 4, :]), P.ld)
            for sub in range(4):
                n = 4 * g + sub
                E.wait(P.fin, self.b("fin") + n + 1)
                E.inc(E.dma_start(out=P.dram["xa"][:, n, :],
                                  in_=self.xa_t[:, sub % 2, :]), P.st)
        if P.mode_io.get("stats") != "in":
            E.wait(P.fin, self.b("fin") + 4 * self.NG + 1)
            E.inc(E.dma_start(out=P.dram["stats"][:], in_=P.sb["stat"][:]), P.st)
            E.wait(P.st, self.b("st") + 16 * (4 * self.NG + 1))
        else:
            E.wait(P.st, self.b("st") + 16 * 4 * self.NG)

    def st_xa(self, n):
        return self.b("st") + 16 * (n + 1)

    def e_tensor(self, E):
        P = self.P
        ones = P.sb["ones_bf"][:, 0:128]
        ident = P.sb["ident"][:, 0:128]
        pT = [P.ps["pT0"], P.ps["pT1"]]
        psA, psB, psC, psD = (P.ps["psA"], P.ps["psB"], P.ps["psC"], P.ps["psD"])
        pC = [psC[:, 0:512], psC[:, 512:1024]]
        E.wait(P.wl, self.b("wl") + 16 * 5)
        E.wait(P.ld, self.b("ld") + 16 * 5)
        for g in range(self.NG):
            tpb = self.b("tp") + g * self.TP_G
            teb = self.b("te") + g * self.TE_G
            pe1b = self.b("pe1") + g * self.PE1_G
            ev1b = self.b("ev1") + g * self.EV1_G
            pe2b = self.b("pe2") + g * self.PE2_G
            pgb = self.b("pg") + g * self.PG_G
            gcb = self.b("gc") + g * self.GC_G
            ev2b = self.b("ev2") + g * self.EV2_G
            epb = self.b("ep") + g * self.EP_G
            for sub in range(4):
                ks = g * self.TP_G + sub
                E.wait(P.aff, self.b("aff") + g * self.AFF_G + sub + 1)
                if ks >= 2:
                    E.wait(P.te, self.b("te") + self.te_before(ks - 1))
                for c in range(NDC):
                    mm = E.op("transpose", pT[ks % 2][:, c * 128:(c + 1) * 128],
                              self.h2t[:, sub, c * 128:(c + 1) * 128], ident)
                E.inc(mm, P.tp)
            E.wait(P.te, teb + 4)
            for j in range(24):
                u = g * self.PE1_G + j
                bank = psA if u % 2 == 0 else psB
                if u >= 2:
                    E.wait(P.ev1, self.ev1_release(u - 2))
                for c in range(NDC):
                    mm = E.op("matmul", bank[:],
                              self.wqkv[:, c, j * 128:(j + 1) * 128],
                              self.hf[:, c, :],
                              start=(c == 0), stop=(c == NDC - 1))
                E.inc(mm, P.pe1)
            for sub in range(4):
                ks = g * self.TP_G + 4 + sub
                E.wait(P.ev1, ev1b + 24)
                if ks >= 2:
                    E.wait(P.te, self.b("te") + self.te_before(ks - 1))
                for c in range(NDC):
                    mm = E.op("transpose", pT[ks % 2][:, c * 128:(c + 1) * 128],
                              self.qkf[:, 16 + c, sub * 128:(sub + 1) * 128], ident)
                E.inc(mm, P.tp)
            for b in range(8):
                bg = g * 8 + b
                if bg >= 1:
                    E.wait(P.gc, self.b("gc") + bg)
                E.wait(P.ev1, ev1b + 16)
                if b == 0:
                    E.wait(P.gv, self.b("gv") + (1 if g == 0 else g + 1))
                for h in range(NHEAD):
                    hb = (h % 2) * 64
                    lhs = self.qkf[hb:hb + 64, h // 2, b * 64:(b + 1) * 64]
                    if b == 0:
                        rhs_prev = self.ktail[hb:hb + 64, h // 2, :]
                    else:
                        rhs_prev = self.qkf[hb:hb + 64, 8 + h // 2,
                                            (b - 1) * 64:b * 64]
                    rhs_cur = self.qkf[hb:hb + 64, 8 + h // 2, b * 64:(b + 1) * 64]
                    if "safescore" in self.DBG:
                        rhs_prev = rhs_cur
                    fo = (h // 2) * 128
                    po, pc_ = (0, 64) if b % 2 == 1 else (64, 0)
                    mm = E.op("matmul", psC[hb:hb + 64, fo + po:fo + po + 64],
                              lhs, rhs_prev, start=True, stop=True,
                              tile_position=(hb, hb))
                    mm = E.op("matmul", psC[hb:hb + 64, fo + pc_:fo + pc_ + 64],
                              lhs, rhs_cur, start=True, stop=True,
                              tile_position=(hb, hb))
                E.inc(mm, P.pe2)
                E.wait(P.ep, epb + b + 1)
                ks = g * self.TP_G + 8 + b
                if True:
                    E.wait(P.te, self.b("te") + self.te_before(ks - 1))
                for hg in range(8):
                    srcT = (self.hf[:, hg, 0:128] if "safeat" in self.DBG
                            else self.atb[:, hg, :])
                    mm = E.op("transpose", pT[ks % 2][:, hg * 128:(hg + 1) * 128],
                              srcT, ident)
                E.inc(mm, P.tp)
                E.wait(P.te, teb + 8 + b + 1)
                if b // 2 >= 1 or g >= 1:
                    prevpair = g * 4 + b // 2 - 1
                    E.wait(P.ev2, self.ev2_obcopy(prevpair))
                cb = (b % 2) * 64
                for h in range(NHEAD):
                    aT = self.atT[:, h // 2, (h % 2) * 64:(h % 2) * 64 + 64]
                    od = psD[cb:cb + 64, h * 64:(h + 1) * 64]
                    rhs = (self.vt[:, b // 2, h * 64:(h + 1) * 64] if b % 2 == 1
                           else self.vw[:, h * 64:(h + 1) * 64])
                    mm = E.op("matmul", od, aT, rhs, start=True, stop=True,
                              tile_position=(0, cb))
                E.inc(mm, P.pg)
            for sub in range(4):
                ks = g * self.TP_G + 16 + sub
                E.wait(P.ev2, ev2b + sub + 1)
                E.wait(P.te, self.b("te") + self.te_before(ks - 1))
                for c in range(NDC):
                    mm = E.op("transpose", pT[ks % 2][:, c * 128:(c + 1) * 128],
                              self.obt[:, sub, c * 128:(c + 1) * 128], ident)
                E.inc(mm, P.tp)
            for sub in range(4):
                for hf_ in range(2):
                    u = g * self.PE1_G + 24 + sub * 2 + hf_
                    bank = psA if u % 2 == 0 else psB
                    r2 = (u - 2) % self.PE1_G
                    if r2 < 24:
                        E.wait(P.ev1, self.b("ev1")
                               + ((u - 2) // self.PE1_G) * self.EV1_G + r2 + 1)
                    else:
                        E.wait(P.ev2, self.b("ev2")
                               + ((u - 2) // self.PE1_G) * self.EV2_G
                               + 4 + (r2 - 24) + 1)
                    E.wait(P.te, teb + 16 + sub + 1)
                    wsrc = self.hf if "safewout" in self.DBG else self.obf
                    for c in range(NDC):
                        mm = E.op("matmul", bank[:],
                                  wsrc[:, c, sub * 128:(sub + 1) * 128],
                                  self.wout[:, c, hf_ * 512:(hf_ + 1) * 512],
                                  start=(c == 0), stop=False)
                    mm = E.op("matmul", bank[:], ones,
                              self.borow[0:1, 0, hf_ * 512:(hf_ + 1) * 512],
                              start=False, stop=True)
                    E.inc(mm, P.pe1)

    def te_before(self, ks):
        g, r = divmod(ks, self.TP_G)
        return g * self.TE_G + r + 1

    def ev1_release(self, u):
        g, r = divmod(u, self.PE1_G)
        if r < 24:
            return self.b("ev1") + g * self.EV1_G + r + 1
        return self.b("ev1") + g * self.EV1_G + 24

    def ev2_obcopy(self, pair):
        g, p = divmod(pair, 4)
        return self.b("ev2") + g * self.EV2_G + p + 1

    def e_scalar(self, E):
        P = self.P
        for g in range(self.NG):
            pe1b = self.b("pe1") + g * self.PE1_G
            axb = self.b("ax") + g * self.AX_G
            for j in range(24):
                u = g * self.PE1_G + j
                E.wait(P.pe1, pe1b + j + 1)
                if g >= 1:
                    E.wait(P.gv, self.b("gv") + g + 1)
                bank = P.ps["psA"] if u % 2 == 0 else P.ps["psB"]
                if j < 8:
                    bias = self.bq8[:, j:j + 1]
                    sc = P.cf(ATT_SCALE)
                else:
                    bias = self.bqs[:, j:j + 1]
                    sc = 1.0
                E.inc(E.op("activation", self.qkf[:, j, :], bank[:],
                           AF.Identity, scale=sc, bias=bias), P.ev1)
            for b in range(8):
                E.wait(P.ax, axb + b + 1)
                if g * 8 + b >= 1:
                    E.wait(P.tp, self.b("tp") + self.tp_at(g * 8 + b - 1))
                fexp = AF.Identity if "safeexp" in self.DBG else AF.Exp
                for hg in range(8):
                    mm = E.op("activation", self.atb[:, hg, :],
                              self.scf[:, hg, :], fexp,
                              bias=self.nm[:, hg:hg + 1])
                E.incd(P.ge)
            for sub in range(4):
                n = 4 * g + sub
                E.wait(P.ax, axb + 8 + sub + 1)
                E.inc(E.op("activation", self.stb[:, 6:7], self.stb[:, 5:6],
                           AF.Sqrt, bias=P.cf(EPS0)), P.ge)

    def tp_at(self, bg):
        g, b = divmod(bg, 8)
        return g * self.TP_G + 8 + b + 1

    def e_vector(self, E):
        P = self.P
        pT = [P.ps["pT0"], P.ps["pT1"]]
        psA, psB, psC, psD = (P.ps["psA"], P.ps["psB"], P.ps["psC"], P.ps["psD"])
        E.wait(P.ld, self.b("ld") + 16 * 5)
        E.op("tensor_scalar_mul", self.bq8[:], self.bqs[:, 0:8], ATT_SCALE)
        E.op("memset", self.ktail[:], 0.0)
        E.inc(E.op("memset", self.vtail[:], 0.0), P.gv)
        for g in range(self.NG):
            teb = self.b("te") + g * self.TE_G
            tpb = self.b("tp") + g * self.TP_G
            pe2b = self.b("pe2") + g * self.PE2_G
            pgb = self.b("pg") + g * self.PG_G
            ev2b = self.b("ev2") + g * self.EV2_G
            axb = self.b("ax") + g * self.AX_G
            geb = self.b("ge") + g * self.GE_G
            epb = self.b("ep") + g * self.EP_G
            E.wait(P.ld, self.b("ld") + 16 * (3 + 2 * (g + 1)))
            for sub in range(4):
                if g >= 1:
                    E.wait(P.tp, tpb - 16)
                E.op("tensor_mul", self.yt2[:], self.xht[:, sub, :], self.l1gb[:])
                E.inc(E.op("tensor_add", self.h2t[:, sub, :], self.yt2[:],
                           self.l1bb[:]), P.aff)
            for sub in range(4):
                ks = g * self.TP_G + sub
                E.wait(P.tp, self.b("tp") + self.tp_done(ks))
                E.inc(E.op("tensor_copy",
                           self.hf[:, :, sub * 128:(sub + 1) * 128],
                           pT[ks % 2].rearrange("p (c w) -> p c w", c=NDC)), P.te)
            for sub in range(4):
                ks = g * self.TP_G + 4 + sub
                E.wait(P.tp, self.b("tp") + self.tp_done(ks))
                E.inc(E.op("tensor_copy",
                           self.vt[:, sub, :].rearrange("p (c w) -> p c w", c=NDC),
                           pT[ks % 2].rearrange("p (c w) -> p c w", c=NDC)), P.te)
            for b in range(8):
                bg = g * 8 + b
                E.wait(P.pe2, pe2b + b + 1)
                mk = self.mask if b % 2 == 1 else self.mask2
                E.inc(E.op("tensor_add", self.scf[:], psC.rearrange(
                    "p (a w) -> p a w", a=8), mk[:]), P.gc)
                E.inc(E.op("reduce_max", self.nm[:], self.scf[:], axis=AX.X,
                           negate=True), P.ax)
                E.wait(P.ge, geb + b + 1)
                E.op("reduce_sum", self.rr[:], self.atb[:], axis=AX.X)
                E.op("reciprocal", self.rr[:], self.rr[:])
                for hg in range(8):
                    mm = E.op("tensor_scalar", self.atb[:, hg, :],
                              self.atb[:, hg, :], self.rr[:, hg:hg + 1],
                              None, ALU.mult)
                E.inc(mm, P.ep)
                if b % 2 == 0:
                    pv = (self.vtail[64:128, :] if b == 0
                          else self.vt[64:128, (b - 1) // 2, :])
                    E.op("tensor_copy", self.vw[64:128, :], pv)
                    E.op("tensor_copy", self.vw[0:64, :], self.vt[0:64, b // 2, :])
                ks = g * self.TP_G + 8 + b
                E.wait(P.tp, self.b("tp") + self.tp_done(ks))
                E.inc(E.op("tensor_copy", self.atT[:],
                           pT[ks % 2].rearrange("p (h w) -> p h w", h=8)), P.te)
                if b % 2 == 1:
                    E.wait(P.pg, pgb + b + 1)
                    E.inc(E.op("tensor_copy", self.obt[:, b // 2, :],
                               psD[:]), P.ev2)
            for sub in range(4):
                ks = g * self.TP_G + 16 + sub
                E.wait(P.tp, self.b("tp") + self.tp_done(ks))
                E.inc(E.op("tensor_copy",
                           self.obf[:, :, sub * 128:(sub + 1) * 128],
                           pT[ks % 2].rearrange("p (c w) -> p c w", c=NDC)), P.te)
            E.op("tensor_copy", self.ktail[:], self.qkf[:, 8:16, 448:512])
            E.inc(E.op("tensor_copy", self.vtail[:], self.vt[:, 3, :]), P.gv)
            for sub in range(4):
                n = 4 * g + sub
                p = sub % 2
                for hf_ in range(2):
                    u = g * self.PE1_G + 24 + sub * 2 + hf_
                    E.wait(P.pe1, self.b("pe1") + u + 1)
                    bank = psA if u % 2 == 0 else psB
                    E.op("scalar_tensor_tensor",
                         self.yt2[:, hf_ * 512:(hf_ + 1) * 512],
                         self.bxt[:, sub, hf_ * 512:(hf_ + 1) * 512],
                         stat(P, ST_R1, n), bank[:],
                         op0=ALU.mult, op1=ALU.add)
                    E.inc(E.op("tensor_scalar",
                               self.yt2[:, hf_ * 512:(hf_ + 1) * 512],
                               self.yt2[:, hf_ * 512:(hf_ + 1) * 512],
                               stat(P, ST_NB1, n), None, ALU.add), P.ev2)
                if n >= 2:
                    E.wait(P.st, self.st_xa(n - 1))
                E.inc(E.op("tensor_copy", self.xa_t[:, p, :], self.yt2[:]), P.fin)
                E.op("reduce_sum", self.stb[:, 4:5], self.yt2[:], axis=AX.X)
                E.op("tensor_mul", self.yt2[:], self.yt2[:], self.yt2[:])
                E.op("reduce_sum", self.stb[:, 5:6], self.yt2[:], axis=AX.X)
                E.op("tensor_scalar_mul", self.stb[:, 4:5], self.stb[:, 4:5], 1.0 / D)
                E.op("tensor_scalar_mul", self.stb[:, 5:6], self.stb[:, 5:6], 1.0 / D)
                E.op("tensor_mul", self.stb[:, 7:8], self.stb[:, 4:5],
                     self.stb[:, 4:5])
                E.inc(E.op("tensor_sub", self.stb[:, 5:6], self.stb[:, 5:6],
                           self.stb[:, 7:8]), P.ax)
                E.wait(P.ge, geb + 8 + sub + 1)
                E.op("reciprocal", stat(P, ST_RXA, n), self.stb[:, 6:7])
                E.op("tensor_scalar", stat(P, ST_NBXA, n), self.stb[:, 4:5],
                     stat(P, ST_RXA, n), -1.0, ALU.mult, ALU.mult)
        E.inc(E.op("memset", self.stb[:, 15:16], 0.0), P.fin)

    def tp_done(self, ks):
        g, r = divmod(ks, self.TP_G)
        return g * self.TP_G + r + 1


def kernel(**inputs):
    ins = {k: np.asarray(v) for k, v in inputs.items()}
    g = host_inputs(ins)
    x = np.ascontiguousarray(ins["L0_attn_x"], dtype=np.float32)
    P = build_program([lambda P: PH0(P), lambda P: MoE(P, "l0"),
                       lambda P: PH2(P), lambda P: MoE(P, "l1"),
                       lambda P: PH4(P), lambda P: MoE(P, "mlp")], {})
    in_maps = []
    for c in range(8):
        m = dict(g)
        m["x_in"] = lay_tm(x[c])
        in_maps.append(m)
    res = run_bass_kernel_spmd(P.nc, in_maps, core_ids=list(range(8)))
    out = np.stack([unlay_tm(res.results[c]["out"]) for c in range(8)])
    return np.ascontiguousarray(out, dtype=np.float32)


# version 03
# speedup vs baseline: 1.0909x; 1.0909x over previous
import numpy as np
import ml_dtypes
import concourse.bass as bass
import concourse.mybir as mybir
from concourse.bass_utils import run_bass_kernel_spmd

F32 = mybir.dt.float32
BF16 = mybir.dt.bfloat16
AF = mybir.ActivationFunctionType
ALU = mybir.AluOpType
AX = mybir.AxisListType
BF = ml_dtypes.bfloat16

N = 4096
D = 1024
NT = N // 128
NDC = D // 128
FF = 4096
FQ = 1024
NFQ = FQ // 128
NQ = FF // FQ
NEXP = 8
TT = 1024
NTILE = N // TT
NSUB = TT // 128
NTH = TT // 512
G1 = NFQ * NTH
G2 = NSUB * 2
NHEAD = 16
DH = 64

EPS0 = 1e-5
EPSQK = 1e-6
EPSG = 1e-3
ATT_SCALE = 0.125

class SC:
    def __init__(self, name, step=1):
        self.name, self.step, self.h, self.cur = name, step, None, 0


class MockIns:
    def then_inc(self, h, n):
        return self


class Eng:

    def __init__(self, eng, real, auto_drain=False):
        self.e, self.real, self.auto_drain = eng, real, auto_drain

    def wait(self, sc, v):
        if self.real:
            self.e.wait_ge(sc.h, v)

    def inc(self, ins, sc, n=1):
        sc.cur += n * sc.step
        if self.real:
            ins.then_inc(sc.h, n * sc.step)
        return ins

    def op(self, name, *a, nodrain=False, **kw):
        if self.real:
            r = getattr(self.e, name)(*a, **kw)
            if self.auto_drain and not nodrain and name != "dma_start":
                self.e.drain()
            return r
        return MockIns()

    def dr(self):
        if self.real:
            self.e.drain()

    def incd(self, sc, n=1):
        sc.cur += n * sc.step
        if self.real:
            self.e.drain()
            self.e.sem_inc(sc.h, n * sc.step)

    def __getattr__(self, name):
        def call(*a, **kw):
            return self.op(name, *a, **kw)
        return call

    def done(self, P):
        P.pb.cur += 1
        if self.real:
            self.e.drain()
            self.e.sem_inc(P.pb.h, 1)


SEMS = [("ld", 16), ("wl", 16), ("st", 16),
        ("gwl", 16), ("wlA", 16), ("wlB", 16), ("b2l", 16), ("ldE", 16),
        ("tp", 1), ("te", 1), ("pg", 1), ("gc", 1), ("ge", 1), ("gv", 1), ("aff", 1),
        ("pe1", 1), ("ev1", 1), ("pe2", 1), ("ev2", 1),
        ("ax", 1), ("ep", 1), ("fin", 1), ("pb", 1)]

ENGINES = ("gpsimd", "tensor", "scalar", "vector", "sync")

SB_BASE = 16512
SB_STAT = 0
SB_GATES = 1536
SB_MISC = 2560
SB_XF = 8 * 1024
SB_XE = 24 * 1024
SB_H = 40 * 1024
SB_Y = 56 * 1024
SB_WA = 88 * 1024
SB_WB = 120 * 1024
SB_FREE = 152 * 1024
SB_END = 212832


class Prog:
    def __init__(self):
        self.nc = bass.Bass("TRN2", target_bir_lowering=False,
                            dynamic_dma_scratch_size=16384)
        self.sc = {}
        for nm, st in SEMS:
            s = SC(nm, st)
            self.sc[nm] = s
            setattr(self, nm, s)
        self.dram = {}
        self.sb = {}
        self.ps = {}
        self.consts = {}

    def din(self, name, shape, dtype=F32):
        self.dram[name] = self.nc.dram_tensor(name, list(shape), dtype,
                                              kind="ExternalInput").ap()
        return self.dram[name]

    def dout(self, name, shape, dtype=F32):
        self.dram[name] = self.nc.dram_tensor(name, list(shape), dtype,
                                              kind="ExternalOutput").ap()
        return self.dram[name]

    def dtmp(self, name, shape, dtype=F32):
        self.dram[name] = self.nc.dram_tensor(name, list(shape), dtype,
                                              kind="Internal").ap()
        return self.dram[name]

    def const(self, name, arr):
        if name not in self.consts:
            self.consts[name] = self.nc.inline_tensor(
                np.ascontiguousarray(arr), name=name).ap()
        return self.consts[name]

    def sbuf(self, name, off, shape, dtype):
        if name not in self.sb:
            self.sb[name] = self.nc.alloc_sbuf_tensor_at(
                name, list(shape), dtype, offset=SB_BASE + off).ap()
        return self.sb[name]

    def fb(self, name, shape, dtype):
        import math
        nbytes = int(np.prod(shape[1:])) * (2 if dtype == BF16 else 4)
        nbytes = (nbytes + 511) // 512 * 512
        cur = getattr(self, "_fcur", 0)
        assert SB_FREE + cur + nbytes <= SB_END, (name, cur, nbytes)
        ap = self.sbuf(name, SB_FREE + cur, shape, dtype)
        self._fcur = cur + nbytes
        return ap

    def fb_at(self, name, off, shape, dtype):
        return self.sbuf(name, SB_FREE + off, shape, dtype)

    def cf(self, value):
        if not hasattr(self, "cfvals"):
            self.cfvals = []
        if value not in self.cfvals:
            self.cfvals.append(value)
        i = self.cfvals.index(value)
        assert i < 8
        return self.sb["cst"][:, i:i + 1]


class Phase:
    name = "ph"

    def __init__(self, P):
        self.P = P
        self.idx = None
        self.bases = {}

    def setup(self):
        pass

    def b(self, nm):
        return self.bases[nm]

    def emit(self, eng_name, E):
        E.wait(self.P.pb, 5 * self.idx)
        fn = getattr(self, "e_" + eng_name, None)
        if fn is not None:
            fn(E)
        E.done(self.P)


def build_program(phases_cls, mode_io):
    P = Prog()
    P.mode_io = mode_io
    declare_tensors(P)
    phases = []
    for f in phases_cls:
        ph = f(P)
        phases.append(ph)
    for ph in phases:
        ph.setup()
    for i, ph in enumerate(phases):
        ph.idx = i
        ph.bases = {nm: P.sc[nm].cur for nm in P.sc}
        for en in ENGINES:
            ph.emit(en, Eng(None, False))
    totals = {nm: P.sc[nm].cur for nm in P.sc}
    for nm in P.sc:
        P.sc[nm].cur = 0
    nc = P.nc
    import contextlib
    with contextlib.ExitStack() as stack:
        for nm, _ in SEMS:
            P.sc[nm].h = stack.enter_context(nc.semaphore(nm))
        with nc.Block() as blk:
            def mk(en):
                def run(eng):
                    ad = en in ("vector",)
                    for ph in phases:
                        ph.emit(en, Eng(eng, True, auto_drain=ad))
                return run
            blk.gpsimd(mk("gpsimd"))
            blk.tensor(mk("tensor"))
            blk.scalar(mk("scalar"))
            blk.vector(mk("vector"))
            blk.sync(mk("sync"))
    for nm in P.sc:
        assert P.sc[nm].cur == totals[nm], (nm, P.sc[nm].cur, totals[nm])
    return P


def declare_tensors(P):
    io = getattr(P, "mode_io", {})

    def scratch(name, shape, dtype):
        k = io.get(name)
        if k == "in":
            return P.din(name, shape, dtype)
        if k == "out":
            return P.dout(name, shape, dtype)
        return P.dtmp(name, shape, dtype)

    P.din("x_in", [128, NT, D], F32)
    for w in ("wq", "wk", "wv", "wq2", "wk2", "wv2", "wo"):
        P.din(w, [128, NDC, D], F32)
    P.din("bqkv1", [128, 3, 64], F32)
    P.din("rows0", [1, 4, D], F32)
    P.din("g0rep", [128, D], F32)
    P.din("be0rep", [128, D], F32)
    P.din("gw0", [128, NDC, 32], F32)
    P.din("w1e0", [NEXP, NQ, 128, NDC, FQ], F32)
    P.din("b1e0", [NEXP, NQ, 128, 128], F32)
    P.din("w2e0", [NEXP, NQ, 128, NFQ, D], F32)
    P.din("b2e0", [NEXP, D], F32)
    P.din("wqkv", [128, NDC, 3 * D], F32)
    P.din("l1grep", [128, D], F32)
    P.din("l1brep", [128, D], F32)
    P.din("bqkv", [128, 128], F32)
    P.din("wout1", [128, NDC, D], F32)
    P.din("rows1", [1, 8, D], F32)
    P.din("g2rep", [128, D], F32)
    P.din("b2rep", [128, D], F32)
    P.din("b2mrow", [1, D], F32)
    P.din("w1m", [NQ, 128, NDC, FQ], F32)
    P.din("b1m", [NQ, 128, 128], F32)
    P.din("w2m", [NQ, 128, NFQ, D], F32)
    P.din("gw1", [128, NDC, 32], F32)
    P.din("rb1", [128, 128], F32)
    P.din("lng", [128, NEXP, 16], F32)
    P.din("lnb", [128, NEXP, 16], F32)
    P.din("w1e1", [NEXP, NQ, 128, NDC, FQ], F32)
    P.din("b1e1", [NEXP, NQ, 128, 128], F32)
    P.din("w2e1", [NEXP, NQ, 128, NFQ, D], F32)
    P.din("b2e1", [NEXP, D], F32)
    scratch("xn0", [128, NT, D], BF16)
    scratch("y0", [128, NT, D], F32)
    scratch("ved", [128, 8, 4, D], BF16)
    scratch("x1", [128, NT, D], F32)
    scratch("xh1", [128, NT, D], BF16)
    scratch("y1", [128, NT, D], F32)
    scratch("xa", [128, NT, D], F32)
    scratch("stats", [128, 12, NT], F32)
    scratch("gdump", [128, NT, 8], F32)
    scratch("ldump", [128, NT, 8], F32)
    scratch("fdump", [128, NTILE, NDC, TT], BF16)
    scratch("sdump", [128, 512], F32)
    scratch("adump", [128, NG2 if False else 8, NDC, 512], BF16)
    scratch("edump", [128, 3, D], F32)
    scratch("tdump", [128, NTILE, NSUB, D], BF16)
    scratch("pdump", [128, 1024], BF16)
    P.dout("out", [128, NT, D], F32)

    P.sbuf("stat", SB_STAT, [128, 12, NT], F32)
    P.sbuf("ones_bf", SB_MISC, [1, 512], BF16)
    P.sbuf("ones_f32", SB_MISC + 1024, [1, 128], F32)
    P.sbuf("ident", SB_MISC + 1536, [128, 256], BF16)
    P.sbuf("cst", SB_MISC + 2048, [128, 8], F32)
    nc = P.nc
    P.ps["psA"] = nc.alloc_psum_tensor("psA", [128, 512], F32).ap()
    P.ps["psB"] = nc.alloc_psum_tensor("psB", [128, 512], F32).ap()
    P.ps["psC"] = nc.alloc_psum_tensor("psC", [128, 1024], F32).ap()
    P.ps["psD"] = nc.alloc_psum_tensor("psD", [128, 1024], F32).ap()
    P.ps["pT0"] = nc.alloc_psum_tensor("pT0", [128, 1024], BF16).ap()
    P.ps["pT1"] = nc.alloc_psum_tensor("pT1", [128, 1024], BF16).ap()


ST_MU0, ST_R0, ST_NB0 = 0, 1, 2
ST_R1, ST_NB1 = 3, 4
ST_RA, ST_NBA, ST_SA = 5, 6, 7
ST_RXA, ST_NBXA = 8, 9


def stat(P, row, n):
    return P.sb["stat"][:, row, n:n + 1]


class PH0(Phase):
    name = "ph0"

    def setup(self):
        P = self.P
        self.xt = P.sbuf("ph0_xt", SB_FREE, [128, 2, D], F32)
        self.sq = P.sbuf("ph0_sq", SB_FREE + 8192, [128, D], F32)
        self.xn = P.sbuf("ph0_xn", SB_FREE + 12288, [128, 2, D], BF16)
        self.t = P.sbuf("ph0_t", SB_FREE + 16384, [128, 16], F32)
        self.c_ones_bf = P.const("c_ones_bf", np.ones((1, 512), dtype=BF))
        self.c_ones_f = P.const("c_ones_f", np.ones((1, 128), dtype=np.float32))
        self.c_ident = P.const("c_ident", np.concatenate([np.eye(128, dtype=BF), np.zeros((128, 128), BF)], 1))

    def e_sync(self, E):
        P = self.P
        E.inc(E.dma_start(out=P.sb["ones_bf"][:], in_=self.c_ones_bf), P.ld)
        E.inc(E.dma_start(out=P.sb["ones_f32"][:], in_=self.c_ones_f), P.ld)
        E.inc(E.dma_start(out=P.sb["ident"][:], in_=self.c_ident), P.ld)
        for n in range(NT):
            if n >= 2:
                E.wait(P.ev2, self.b("ev2") + (n - 1))
            E.inc(E.dma_start(out=self.xt[:, n % 2, :],
                              in_=P.dram["x_in"][:, n, :]), P.ld)
            E.wait(P.ev2, self.b("ev2") + n + 1)
            E.inc(E.dma_start(out=P.dram["xn0"][:, n, :],
                              in_=self.xn[:, n % 2, :]), P.st)
        E.wait(P.fin, self.b("fin") + NT)
        E.inc(E.dma_start(out=P.dram["stats"][:, 0:3, :],
                          in_=P.sb["stat"][:, 0:3, :]), P.st)
        E.wait(P.st, self.b("st") + 16 * (NT + 1))

    def e_vector(self, E):
        P = self.P
        t = self.t
        for i, v in enumerate(P.cfvals):
            ins = E.op("memset", P.sb["cst"][:, i:i + 1], v)
        E.inc(ins, P.gc)
        for n in range(NT):
            i = n % 2
            o = 8 * i
            E.wait(P.ld, self.b("ld") + 16 * (n + 4))
            if n >= 2:
                E.wait(P.ev1, self.b("ev1") + n - 1)
            E.op("reduce_sum", t[:, o:o + 1], self.xt[:, i, :], axis=AX.X)
            E.op("tensor_mul", self.sq[:], self.xt[:, i, :], self.xt[:, i, :])
            E.op("reduce_sum", t[:, o + 1:o + 2], self.sq[:], axis=AX.X)
            E.op("tensor_scalar_mul", stat(P, ST_MU0, n), t[:, o:o + 1], 1.0 / D)
            E.op("tensor_scalar_mul", t[:, o + 2:o + 3], t[:, o + 1:o + 2], 1.0 / D)
            E.op("tensor_mul", t[:, o + 3:o + 4], stat(P, ST_MU0, n),
                 stat(P, ST_MU0, n))
            E.inc(E.op("tensor_sub", t[:, o + 4:o + 5], t[:, o + 2:o + 3],
                       t[:, o + 3:o + 4]), P.ax)
            E.wait(P.ev1, self.b("ev1") + n + 1)
            E.op("reciprocal", stat(P, ST_R0, n), t[:, o + 6:o + 7])
            E.inc(E.op("tensor_scalar", stat(P, ST_NB0, n), stat(P, ST_MU0, n),
                       stat(P, ST_R0, n), -1.0, ALU.mult, ALU.mult), P.fin)

    def e_scalar(self, E):
        P = self.P
        t = self.t
        for n in range(NT):
            o = 8 * (n % 2)
            E.wait(P.ax, self.b("ax") + n + 1)
            if n == 0:
                E.wait(P.gc, self.b("gc") + 1)
            E.inc(E.op("activation", t[:, o + 6:o + 7], t[:, o + 4:o + 5], AF.Sqrt,
                       bias=P.cf(EPS0)), P.ev1)
            E.wait(P.fin, self.b("fin") + n + 1)
            if n >= 2:
                E.wait(P.st, self.b("st") + 16 * n)
            E.inc(E.op("activation", self.xn[:, n % 2, :], self.xt[:, n % 2, :],
                       AF.Identity, scale=stat(P, ST_R0, n),
                       bias=stat(P, ST_NB0, n)), P.ev2)



class MoE(Phase):

    def __init__(self, P, kind):
        super().__init__(P)
        self.kind = kind
        self.name = "moe" + kind
        self.nexp = 1 if kind == "mlp" else NEXP
        self.act = AF.Relu if kind == "l0" else AF.Gelu
        self.has_gates = kind in ("l0", "l1")

    def setup(self):
        P = self.P
        k = self.kind
        pre = self.name
        self.in_d = {"l0": P.dram["xn0"], "l1": P.dram["xh1"],
                     "mlp": P.dram["xa"]}[k]
        self.out_d = {"l0": P.dram["y0"], "l1": P.dram["y1"],
                      "mlp": P.dram["out"]}[k]
        wn = {"l0": ("w1e0", "b1e0", "w2e0", "b2e0"),
              "l1": ("w1e1", "b1e1", "w2e1", "b2e1"),
              "mlp": ("w1m", "b1m", "w2m", "b2mrow")}[k]
        self.w1_d, self.b1_d, self.w2_d = (P.dram[wn[0]], P.dram[wn[1]],
                                           P.dram[wn[2]])
        self.b2_d = P.dram[wn[3]]
        self.xf = P.sbuf(pre + "_xf", SB_XF, [128, NDC, TT], BF16)
        self.y = P.sbuf(pre + "_y", SB_Y, [128, NSUB, D], F32)
        self.w1s = P.sbuf(pre + "_w1s", SB_WA, [128, 2, NDC, FQ], BF16)
        self.w2s = P.sbuf(pre + "_w2s", SB_WB, [128, 2, NFQ, D], BF16)
        P._fcur = 0
        self.b1s = P.fb(pre + "_b1s", [128, 2, 128], F32)
        self.b2s = P.fb(pre + "_b2s", [1, 1 if k == "mlp" else 2, D], BF16)
        if k == "mlp":
            self.xa_t = P.sbuf(pre + "_xat", SB_XE, [128, NSUB, D], F32)
            self.h = P.fb(pre + "_h", [128, NFQ, TT], BF16)
            self.xt = P.fb(pre + "_xt", [128, NSUB, D], BF16)
            xt_off = P._fcur - 16384
            self.g2b = P.fb(pre + "_g2b", [128, D], F32)
            self.b2b = P.fb(pre + "_b2b", [128, D], F32)
            self.eb = P.fb(pre + "_eb", [128, 2, D], F32)
            self.et = P.fb_at(pre + "_et", xt_off, [128, 2, D], F32)
            self.eb2 = P.fb_at(pre + "_eb2", xt_off + 8192, [128, 2, D], F32)
        else:
            self.xe = P.sbuf(pre + "_xe", SB_XE, [128, NDC, TT], BF16)
            self.h = P.sbuf(pre + "_h", SB_H, [128, NFQ, TT], BF16)
            self.xt = P.fb(pre + "_xt", [128, NSUB, D], BF16)
            self.gw = P.fb(pre + "_gw", [128, NDC, 32], BF16)
            self.gs = P.sbuf(pre + "_gs", SB_GATES, [128, NT, 8], F32)
            self.lgr = P.fb(pre + "_lgr", [128, 8], F32)
            self.gt = P.fb(pre + "_gt", [128, 16], F32)
            if k == "l1":
                self.lngs = P.fb(pre + "_lng", [128, NEXP, 16], F32)
                self.lnbs = P.fb(pre + "_lnb", [128, NEXP, 16], F32)
                self.rbbf = P.fb(pre + "_rbb", [128, 128], F32)
                self.eb = P.fb(pre + "_eb", [128, 2, D], F32)
                self.et = P.fb(pre + "_et", [128, 2, D], F32)
        if P.mode_io.get("ldump") == "out" and k != "mlp":
            self.ldmp = P.fb(pre + "_ldmp", [128, NT, 8], F32)

    def nst(self):
        return NTILE * self.nexp * NQ

    def wpre(self):
        return 1

    def wl_cum(self, s):
        b2n = 0 if self.kind == "mlp" else (s + NQ - 1) // NQ
        return self.wpre() + 3 * s + b2n

    def wl_w1(self, s):
        b2 = 1 if (s % NQ == 0 and self.kind != "mlp") else 0
        return 16 * (self.wl_cum(s) + b2 + 2)

    def wl_w2(self, s):
        b2 = 1 if (s % NQ == 0 and self.kind != "mlp") else 0
        return 16 * (self.wl_cum(s) + b2 + 3)

    def wl_b2(self, ge):
        if self.kind == "mlp":
            return 16 * 1
        s = ge * NQ
        return 16 * (self.wl_cum(s) + 1)

    def lpre(self):
        return {"l0": 0, "l1": 3, "mlp": 2}[self.kind]

    def ld_tile(self):
        return 1

    def ld_in(self, t):
        return self.b("ld") + 16 * (self.lpre() + t + 1)

    def ld_epi(self, t, sub, nper):
        return self.b("ldE") + 16 * nper * (t * NSUB + sub + 1)

    def e_gpsimd(self, E):
        P = self.P
        if self.has_gates:
            E.inc(E.dma_start(out=self.gw[:],
                              in_=P.dram["gw0" if self.kind == "l0" else "gw1"]), P.gwl)
        else:
            E.inc(E.dma_start(out=self.b2s[0:1, 0, :], in_=self.b2_d), P.b2l)
        for s in range(self.nst()):
            ge = s // NQ
            e = ge % self.nexp
            q = s % NQ
            slot = s % 2
            if q == 0 and self.kind != "mlp":
                if ge >= 1:
                    E.wait(P.pe2, self.b("pe2") + (ge - 1) * NQ * G2 + G2)
                E.inc(E.dma_start(out=self.b2s[0:1, ge % 2, :],
                                  in_=self.b2_d[e:e + 1, :]), P.b2l)
            if s >= 1:
                E.wait(P.pe1, self.b("pe1") + (s - 1) * G1 + 1)
            mlp = self.kind == "mlp"
            E.inc(E.dma_start(out=self.w1s[:, slot],
                              in_=self.w1_d[q] if mlp else self.w1_d[e, q]), P.wlA)
            E.inc(E.dma_start(out=self.b1s[:, slot],
                              in_=self.b1_d[q] if mlp else self.b1_d[e, q]), P.wlA)
            if s >= 1:
                E.wait(P.pe2, self.b("pe2") + (s - 1) * G2 + 1)
            E.inc(E.dma_start(out=self.w2s[:, slot],
                              in_=self.w2_d[q] if mlp else self.w2_d[e, q]), P.wlB)

    def e_sync(self, E):
        P = self.P
        k = self.kind
        if k == "l1":
            E.inc(E.dma_start(out=self.rbbf[:], in_=P.dram["rb1"]), P.ld)
            E.inc(E.dma_start(out=self.lngs[:], in_=P.dram["lng"]), P.ld)
            E.inc(E.dma_start(out=self.lnbs[:], in_=P.dram["lnb"]), P.ld)
        elif k == "mlp":
            E.inc(E.dma_start(out=self.g2b[:], in_=P.dram["g2rep"]), P.ld)
            E.inc(E.dma_start(out=self.b2b[:], in_=P.dram["b2rep"]), P.ld)
        for t in range(NTILE):
            if t >= 1:
                E.wait(P.tp, self.b("tp") + t * NSUB)
                if k == "mlp":
                    E.wait(P.fin, self.b("fin") + t * NSUB)
            tgt = self.xa_t if k == "mlp" else self.xt
            E.inc(E.dma_start(out=tgt[:],
                              in_=self.in_d[:, t * NSUB:(t + 1) * NSUB, :]), P.ld)
            if k == "l0":
                E.wait(P.ev2, self.b("ev2") + (t + 1) * self.nexp * NQ * G2)
            else:
                src1 = P.dram["x1"] if k == "l1" else P.dram["y1"]
                E.wait(P.tp, self.b("tp") + (t + 1) * NSUB)
                for sub in range(NSUB):
                    j = t * NSUB + sub
                    if j >= 2:
                        E.wait(P.fin, self.b("fin") + j - 1)
                    if j >= 1:
                        E.wait(P.ep, self.b("ep") + j)
                    E.inc(E.dma_start(out=self.eb[:, sub % 2, :],
                                      in_=src1[:, j, :]), P.ldE)
                    if k == "mlp":
                        E.inc(E.dma_start(out=self.eb2[:, sub % 2, :],
                                          in_=P.dram["x1"][:, j, :]), P.ldE)
                E.wait(P.fin, self.b("fin") + (t + 1) * NSUB)
            E.inc(E.dma_start(out=self.out_d[:, t * NSUB:(t + 1) * NSUB, :],
                              in_=self.y[:]), P.st)
        nst_ = NTILE
        if self.has_gates and self.P.mode_io.get("gdump") == "out":
            E.inc(E.dma_start(out=P.dram["gdump"][:], in_=self.gs[:]), P.st)
            nst_ += 1
        if self.has_gates and self.P.mode_io.get("ldump") == "out":
            E.inc(E.dma_start(out=P.dram["ldump"][:], in_=self.ldmp[:]), P.st)
            nst_ += 1
        E.wait(P.st, self.b("st") + 16 * nst_)

    def e_tensor(self, E):
        P = self.P
        ones = P.sb["ones_bf"][:, 0:128]
        ident = P.sb["ident"][:, 0:128]
        pT = [P.ps["pT0"], P.ps["pT1"]]
        psA, psB, psC, psD = (P.ps["psA"], P.ps["psB"], P.ps["psC"], P.ps["psD"])
        pC = [psC[:, 0:512], psC[:, 512:1024]]
        for t in range(NTILE):
            E.wait(P.ld, self.ld_in(t))
            for sub in range(NSUB):
                kk = t * NSUB + sub
                if kk >= 2:
                    E.wait(P.te, self.b("te") + kk - 1)
                if self.kind == "mlp":
                    E.wait(P.aff, self.b("aff") + kk + 1)
                for c in range(NDC):
                    mm = E.op("transpose", pT[kk % 2][:, c * 128:(c + 1) * 128],
                              self.xt[:, sub, c * 128:(c + 1) * 128], ident)
                E.inc(mm, P.tp)
            E.wait(P.te, self.b("te") + (t + 1) * NSUB)
            if self.has_gates:
                E.wait(P.gwl, self.b("gwl") + 16 * self.wpre())
                for sub in range(NSUB):
                    kk = t * NSUB + sub
                    if kk >= 1:
                        E.wait(P.gc, self.b("gc") + kk)
                    for c in range(NDC):
                        mm = E.op("matmul", psD[:, 0:8],
                                  self.xf[:, c, sub * 128:(sub + 1) * 128],
                                  self.gw[:, c, 0:8],
                                  start=(c == 0), stop=(c == NDC - 1))
                    E.inc(mm, P.pg)
            for e in range(self.nexp):
                ge = t * self.nexp + e
                rhs_t = self.xe if self.kind == "l1" else self.xf
                if self.kind == "l1":
                    E.wait(P.aff, self.b("aff") + ge + 1)
                for q in range(NQ):
                    s = ge * NQ + q
                    slot = s % 2
                    E.wait(P.wlA, self.b("wlA") + 16 * (2 * s + 2))
                    g = 0
                    for fc in range(NFQ):
                        for th in range(NTH):
                            gi = s * G1 + g
                            bank = psA if gi % 2 == 0 else psB
                            if gi >= 2:
                                E.wait(P.ev1, self.b("ev1") + gi - 1)
                            for dc in range(NDC):
                                mm = E.op("matmul", bank[:],
                                          self.w1s[:, slot, dc, fc * 128:(fc + 1) * 128],
                                          rhs_t[:, dc, th * 512:(th + 1) * 512],
                                          start=(dc == 0), stop=(dc == NDC - 1))
                            E.inc(mm, P.pe1)
                            g += 1
                    E.wait(P.wlB, self.b("wlB") + 16 * (s + 1))
                    if q == 0:
                        nb2 = 1 if self.kind == "mlp" else ge + 1
                        E.wait(P.b2l, self.b("b2l") + 16 * nb2)
                    E.wait(P.ev1, self.b("ev1") + (s + 1) * G1)
                    g = 0
                    for tt in range(NSUB):
                        for dhh in range(2):
                            gi = s * G2 + g
                            bank = pC[gi % 2]
                            if gi >= 2:
                                E.wait(P.ev2, self.b("ev2") + gi - 1)
                            for fc in range(NFQ):
                                last = (fc == NFQ - 1) and (q != 0)
                                mm = E.op("matmul", bank,
                                          self.h[:, fc, tt * 128:(tt + 1) * 128],
                                          self.w2s[:, slot, fc, dhh * 512:(dhh + 1) * 512],
                                          start=(fc == 0), stop=last)
                            if q == 0:
                                mm = E.op("matmul", bank, ones,
                                          self.b2s[0:1, 0 if self.kind == "mlp" else ge % 2, dhh * 512:(dhh + 1) * 512],
                                          start=False, stop=True)
                            E.inc(mm, P.pe2)
                            g += 1

    def e_scalar(self, E):
        P = self.P
        k = self.kind
        for t in range(NTILE):
            if self.has_gates:
                for sub in range(NSUB):
                    kk = t * NSUB + sub
                    E.wait(P.ax, self.b("ax") + kk + 1)
                    E.op("activation", self.gt[:, 8:16], self.lgr[:],
                         AF.Exp, bias=self.gt[:, 2:3])
                    if k == "l1":
                        E.op("activation", self.gt[:, 4:5], self.gt[:, 1:2],
                             AF.Exp, bias=self.gt[:, 2:3])
                    E.incd(P.ge)
            for e in range(self.nexp):
                ge = t * self.nexp + e
                if k == "l1":
                    E.wait(P.te, self.b("te") + (t + 1) * NSUB)
                    E.wait(P.pe1, self.b("pe1") + ge * NQ * G1)
                    for c in range(NDC):
                        mm = E.op("activation", self.xe[:, c, :], self.xf[:, c, :],
                                  AF.Identity, scale=self.lngs[:, e, c:c + 1],
                                  bias=self.lnbs[:, e, c:c + 1])
                    E.inc(mm, P.aff)
                for q in range(NQ):
                    s = ge * NQ + q
                    for g in range(G1):
                        fc, th = g // NTH, g % NTH
                        gi = s * G1 + g
                        E.wait(P.pe1, self.b("pe1") + gi + 1)
                        bank = P.ps["psA"] if gi % 2 == 0 else P.ps["psB"]
                        E.inc(E.op("activation",
                                   self.h[:, fc, th * 512:(th + 1) * 512],
                                   bank[:], self.act,
                                   bias=self.b1s[:, s % 2, fc:fc + 1]), P.ev1)
            if k == "l1":
                for sub in range(NSUB):
                    j = t * NSUB + sub
                    E.wait(P.ldE, self.ld_epi(t, sub, 1))
                    E.inc(E.op("activation", self.et[:, sub % 2, :],
                               self.eb[:, sub % 2, :], AF.Identity,
                               scale=stat(P, ST_R1, j),
                               bias=stat(P, ST_NB1, j)), P.ep)
            elif k == "mlp":
                for sub in range(NSUB):
                    j = t * NSUB + sub
                    E.wait(P.ldE, self.ld_epi(t, sub, 2))
                    E.wait(P.tp, self.b("tp") + (t + 1) * NSUB)
                    E.wait(P.aff, self.b("aff") + j + 1)
                    E.inc(E.op("activation", self.et[:, sub % 2, :],
                               self.eb[:, sub % 2, :], AF.Exp,
                               scale=self.P.cf(-1.0)), P.ep)

    def e_vector(self, E):
        P = self.P
        k = self.kind
        pT = [P.ps["pT0"], P.ps["pT1"]]
        psC, psD = P.ps["psC"], P.ps["psD"]
        pC = [psC[:, 0:512], psC[:, 512:1024]]
        for t in range(NTILE):
            if t >= 1:
                E.wait(P.st, self.b("st") + 16 * t)
            if k == "mlp":
                E.wait(P.ld, self.ld_in(t))
                for sub in range(NSUB):
                    kk = t * NSUB + sub
                    if kk >= NSUB:
                        E.wait(P.tp, self.b("tp") + kk - NSUB + 1)
                    s1 = self.y[:, 0, :]
                    s2 = self.y[:, 1, :]
                    E.op("scalar_tensor_tensor", s1, self.g2b[:],
                         stat(P, ST_NBXA, kk), self.b2b[:],
                         op0=ALU.mult, op1=ALU.add)
                    E.op("scalar_tensor_tensor", s2, self.xa_t[:, sub, :],
                         stat(P, ST_RXA, kk), self.g2b[:],
                         op0=ALU.mult, op1=ALU.mult)
                    E.inc(E.op("tensor_add", self.xt[:, sub, :], s1, s2), P.aff)
            for sub in range(NSUB):
                kk = t * NSUB + sub
                E.wait(P.tp, self.b("tp") + kk + 1)
                E.inc(E.op("tensor_copy",
                           self.xf[:, :, sub * 128:(sub + 1) * 128],
                           pT[kk % 2].rearrange("p (c w) -> p c w", c=NDC)), P.te)
            if self.has_gates:
                gt = self.gt
                for sub in range(NSUB):
                    kk = t * NSUB + sub
                    E.wait(P.pg, self.b("pg") + kk + 1)
                    E.inc(E.op("tensor_copy", self.lgr[:], psD[:, 0:8]), P.gc)
                    if self.P.mode_io.get("ldump") == "out":
                        E.op("tensor_copy", self.ldmp[:, kk, :], self.lgr[:])
                    if k == "l1":
                        E.op("scalar_tensor_tensor", self.lgr[:], self.lgr[:],
                             stat(P, ST_SA, kk), self.rbbf[:, 0:8],
                             op0=ALU.mult, op1=ALU.add)
                    E.op("reduce_max", gt[:, 0:1], self.lgr[:], axis=AX.X)
                    E.op("tensor_scalar_mul", gt[:, 2:3], gt[:, 0:1], -1.0)
                    E.op("tensor_scalar", gt[:, 8:16], self.lgr[:],
                         gt[:, 0:1], -1e30, ALU.is_ge, ALU.mult)
                    E.op("tensor_add", gt[:, 8:16], gt[:, 8:16], self.lgr[:])
                    E.inc(E.op("reduce_max", gt[:, 1:2], gt[:, 8:16],
                               axis=AX.X), P.ax)
                    E.wait(P.ge, self.b("ge") + kk + 1)
                    if k == "l0":
                        E.op("reduce_sum", gt[:, 3:4], gt[:, 8:16], axis=AX.X)
                    else:
                        E.op("tensor_scalar_add", gt[:, 3:4], gt[:, 4:5], 1.0)
                    E.op("reciprocal", gt[:, 3:4], gt[:, 3:4])
                    E.op("tensor_scalar", gt[:, 8:16], gt[:, 8:16],
                         gt[:, 3:4], None, ALU.mult)
                    E.op("scalar_tensor_tensor", self.gs[:, kk, :],
                         self.lgr[:], gt[:, 1:2], gt[:, 8:16],
                         op0=ALU.is_ge, op1=ALU.mult)
            for e in range(self.nexp):
                for q in range(NQ):
                    s = (t * self.nexp + e) * NQ + q
                    for g in range(G2):
                        tt, dhh = g // 2, g % 2
                        gi = s * G2 + g
                        E.wait(P.pe2, self.b("pe2") + gi + 1)
                        ysl = self.y[:, tt, dhh * 512:(dhh + 1) * 512]
                        bank = pC[gi % 2]
                        if self.has_gates:
                            gAP = self.gs[:, t * NSUB + tt, e:e + 1]
                            if e == 0 and q == 0:
                                ins = E.op("tensor_scalar", ysl, bank, gAP,
                                           None, ALU.mult)
                            else:
                                ins = E.op("scalar_tensor_tensor", ysl, bank,
                                           gAP, ysl, op0=ALU.mult, op1=ALU.add)
                        else:
                            if q == 0:
                                ins = E.op("tensor_copy", ysl, bank)
                            else:
                                ins = E.op("tensor_add", ysl, ysl, bank)
                        E.inc(ins, P.ev2)
            if k == "l1":
                for sub in range(NSUB):
                    j = t * NSUB + sub
                    E.wait(P.ep, self.b("ep") + j + 1)
                    E.inc(E.op("tensor_add", self.y[:, sub, :],
                               self.y[:, sub, :], self.et[:, sub % 2, :]), P.fin)
            elif k == "mlp":
                for sub in range(NSUB):
                    j = t * NSUB + sub
                    E.wait(P.ep, self.b("ep") + j + 1)
                    eb = self.eb[:, sub % 2, :]
                    eb2 = self.eb2[:, sub % 2, :]
                    et = self.et[:, sub % 2, :]
                    E.op("tensor_scalar_add", et, et, 1.0)
                    E.op("reciprocal", et, et)
                    E.op("tensor_mul", et, eb, et)
                    E.op("tensor_add", eb, self.y[:, sub, :],
                         self.xa_t[:, sub, :])
                    E.op("tensor_mul", eb, eb, et)
                    E.inc(E.op("tensor_add", self.y[:, sub, :], eb, eb2), P.fin)


def lay_tm(x):
    return np.ascontiguousarray(x.reshape(NT, 128, -1).transpose(1, 0, 2))


def unlay_tm(x):
    return np.ascontiguousarray(x.transpose(1, 0, 2).reshape(N, -1))


def lay_fm(w):
    K = w.shape[1]
    return np.ascontiguousarray(w.reshape(NDC, 128, K).transpose(1, 0, 2))


def lay_expert_w1(w1):
    sh = w1.shape[:-2]
    r = w1.reshape(sh + (NDC, 128, NQ, FQ))
    order = tuple(range(len(sh))) + (len(sh) + 2, len(sh) + 1, len(sh), len(sh) + 3)
    return np.ascontiguousarray(r.transpose(order))


def lay_expert_b1(b1):
    sh = b1.shape[:-1]
    r = b1.reshape(sh + (NQ, NFQ, 128))
    order = tuple(range(len(sh))) + (len(sh), len(sh) + 2, len(sh) + 1)
    a = np.ascontiguousarray(r.transpose(order))
    out = np.zeros(sh + (NQ, 128, 128), np.float32)
    out[..., :NFQ] = a
    return out


def lay_expert_w2(w2):
    sh = w2.shape[:-2]
    r = w2.reshape(sh + (NQ, NFQ, 128, D))
    order = tuple(range(len(sh))) + (len(sh), len(sh) + 2, len(sh) + 1, len(sh) + 3)
    return np.ascontiguousarray(r.transpose(order))


def host_inputs(ins):
    g = {}
    g["wq"] = lay_fm(ins["L0_attn_Wq"])
    g["wk"] = lay_fm(ins["L0_attn_Wk"])
    g["wv"] = lay_fm(ins["L0_attn_Wv"])
    g["wq2"] = lay_fm(ins["L0_attn_Wq2"])
    g["wk2"] = lay_fm(ins["L0_attn_Wk2"])
    g["wv2"] = lay_fm(ins["L0_attn_Wv2"])
    g["wo"] = lay_fm(ins["L0_attn_Wo"])
    bq = ins["L0_attn_bq"].reshape(NDC, 128).T
    bk = ins["L0_attn_bk"].reshape(NDC, 128).T
    bv = ins["L0_attn_bv"].reshape(NDC, 128).T
    g["bqkv1"] = np.zeros((128, 3, 64), np.float32)
    g["bqkv1"][:, :, 0:NDC] = np.stack([bq, bk, bv], axis=1)
    rows0 = np.zeros((1, 4, D), np.float32)
    rows0[0, 0] = ins["L0_attn_bq2"]
    rows0[0, 1] = ins["L0_attn_bk2"]
    rows0[0, 2] = ins["L0_attn_bv2"]
    rows0[0, 3] = ins["L0_attn_bo"]
    g["rows0"] = rows0
    g["g0rep"] = np.broadcast_to(ins["L0_attn_gamma"][None, :], (128, D)).copy()
    g["be0rep"] = np.broadcast_to(ins["L0_attn_beta"][None, :], (128, D)).copy()
    gw0 = lay_fm(np.ascontiguousarray(ins["L0_ffn_gate_w"].T))
    g["gw0"] = np.zeros((128, NDC, 32), np.float32)
    g["gw0"][:, :, 0:8] = gw0
    g["w1e0"] = lay_expert_w1(ins["L0_ffn_w1"])
    g["b1e0"] = lay_expert_b1(ins["L0_ffn_b1"])
    g["w2e0"] = lay_expert_w2(ins["L0_ffn_w2"])
    g["b2e0"] = ins["L0_ffn_b2"]
    g["wqkv"] = lay_fm(ins["L1_attn_w_qkv"])
    g["l1grep"] = np.broadcast_to(ins["L1_attn_ln1_g"][None, :], (128, D)).copy()
    g["l1brep"] = np.broadcast_to(ins["L1_attn_ln1_b"][None, :], (128, D)).copy()
    g["bqkv"] = np.zeros((128, 128), np.float32)
    g["bqkv"][:, 0:24] = ins["L1_attn_b_qkv"].reshape(24, 128).T
    g["wout1"] = lay_fm(ins["L1_attn_w_out"])
    rows1 = np.zeros((1, 8, D), np.float32)
    rows1[0, 0] = ins["L1_attn_b_out"]
    rows1[0, 1] = ins["L1_attn_ln1_g"]
    rows1[0, 2] = ins["L1_attn_ln1_b"]
    rows1[0, 3] = ins["L1_attn_ln2_g"]
    rows1[0, 4] = ins["L1_attn_ln2_b"]
    rows1[0, 5] = ins["L1_attn_b2"]
    g["rows1"] = rows1
    g["g2rep"] = np.broadcast_to(ins["L1_attn_ln2_g"][None, :], (128, D)).copy()
    g["b2rep"] = np.broadcast_to(ins["L1_attn_ln2_b"][None, :], (128, D)).copy()
    g["b2mrow"] = ins["L1_attn_b2"].reshape(1, D)
    g["w1m"] = lay_expert_w1(ins["L1_attn_w1"])
    g["b1m"] = lay_expert_b1(ins["L1_attn_b1"])
    g["w2m"] = lay_expert_w2(ins["L1_attn_w2"])
    gw1 = lay_fm(ins["L1_ffn_router_W"])
    g["gw1"] = np.zeros((128, NDC, 32), np.float32)
    g["gw1"][:, :, 0:8] = gw1
    rb = np.zeros((128, 128), np.float32)
    rb[:, 0:8] = ins["L1_ffn_router_b"][None, :]
    g["rb1"] = rb
    lng = ins["L1_ffn_ln_g"].reshape(NEXP, NDC, 128).transpose(2, 0, 1)
    g["lng"] = np.zeros((128, NEXP, 16), np.float32)
    g["lng"][:, :, 0:NDC] = lng
    lnb = ins["L1_ffn_ln_b"].reshape(NEXP, NDC, 128).transpose(2, 0, 1)
    g["lnb"] = np.zeros((128, NEXP, 16), np.float32)
    g["lnb"][:, :, 0:NDC] = lnb
    g["w1e1"] = lay_expert_w1(ins["L1_ffn_W1"])
    g["b1e1"] = lay_expert_b1(ins["L1_ffn_b1"])
    g["w2e1"] = lay_expert_w2(ins["L1_ffn_W2"])
    g["b2e1"] = ins["L1_ffn_b2"]
    return {k: np.ascontiguousarray(v, dtype=np.float32) for k, v in g.items()}


NG2 = 8
GS = 512
WQ, WK, WV, WQ2, WK2, WV2, WO = range(7)


class PH2(Phase):
    name = "ph2"
    TP_G = 8; TE_G = 8; PE1_G = 24; EV1_G = 28; PE2_G = 24; EV2_G = 16
    AX_G = 16; GE_G = 16; GV_G = 4; PG_G = 4
    BPE1_G = 8; BEV1_G = 8; BPE2_G = 8; BEV2_G = 8
    BAX_S = 3; BGE_S = 4; BFIN_S = 2

    def setup(self):
        P = self.P
        self.w7 = P.sbuf("ph2_w7", SB_XF, [128, 7, NDC, D], BF16)
        self.axt = P.sbuf("ph2_axt", SB_WB, [128, 4, D], BF16)
        self.axf = P.sbuf("ph2_axf", SB_WB + 8192, [128, NDC, GS], BF16)
        self.p1f = P.sbuf("ph2_p1f", SB_WB + 16384, [128, NDC, GS], BF16)
        self.vsc = P.sbuf("ph2_vsc", SB_WB + 24576, [128, 4, D], BF16)
        self.qe = P.sbuf("ph2_qe", SB_FREE, [128, 4, D], BF16)
        self.ke = P.sbuf("ph2_ke", SB_FREE + 8192, [128, 4, D], BF16)
        self.y0g = P.sbuf("ph2_y0g", SB_FREE, [128, 4, D], F32)
        self.vf = P.sbuf("ph2_vf", SB_FREE + 16384, [128, NDC, GS], BF16)
        self.qk = P.sbuf("ph2_qk", SB_FREE + 24576, [128, D], F32)
        self.eluA = P.sbuf("ph2_eluA", SB_FREE + 28672, [128, GS], F32)
        self.eluB = P.sbuf("ph2_eluB", SB_FREE + 30720, [128, GS], F32)
        self.scrA = P.sbuf("ph2_scrA", SB_FREE + 28672, [128, 2, D], F32)
        self.rowsA = P.sbuf("ph2_rowsA", SB_FREE + 36864, [1, 3, D], BF16)
        self.scrB = P.sbuf("ph2_scrB", SB_FREE + 36864, [128, 2, D], F32)
        self.xhst = P.sbuf("ph2_xhst", SB_FREE + 45056, [128, 2, D], BF16)
        self.g0b = P.sbuf("ph2_g0b", SB_FREE + 49152, [128, D], BF16)
        self.be0b = P.sbuf("ph2_be0b", SB_FREE + 51200, [128, D], BF16)
        self.borow = P.sbuf("ph2_borow", SB_FREE + 53248, [1, 1, D], BF16)
        self.qksr = P.sbuf("ph2_qksr", SB_FREE + 55296, [128, 4, 16], F32)
        self.bq1 = P.sbuf("ph2_bq1", SB_FREE + 55552, [128, 3, 64], F32)
        self.st16 = P.sbuf("ph2_st16", SB_FREE + 56320, [128, 16], F32)
        self.S32 = P.sbuf("ph2_S32", SB_WB, [128, 512], F32)
        self.Sbf = P.sbuf("ph2_Sbf", SB_WB + 2048, [128, 512], BF16)
        self.af = P.sbuf("ph2_af", SB_WB + 4096, [128, NDC, GS], BF16)
        self.bxt = P.sbuf("ph2_bxt", SB_WB + 12288, [128, 4, D], F32)

    def stg(self):
        return 1 if self.P.mode_io.get("adump") == "out" else 0

    def st_x1(self, n):
        return self.b("st") + 16 * (NG2 + (n // 4 + 1) * self.stg() + 2 * n + 1)

    def st_xh(self, n):
        return self.b("st") + 16 * (NG2 + (n // 4 + 1) * self.stg() + 2 * n + 2)

    def e_gpsimd(self, E):
        P = self.P
        for i, w in enumerate(("wq", "wk", "wv", "wq2", "wk2", "wv2", "wo")):
            E.inc(E.dma_start(out=self.w7[:, i], in_=P.dram[w]), P.wl)
        E.inc(E.dma_start(out=self.rowsA[:], in_=P.dram["rows0"][:, 0:3, :]), P.wl)
        E.inc(E.dma_start(out=self.borow[:], in_=P.dram["rows0"][:, 3:4, :]), P.wl)
        E.inc(E.dma_start(out=self.g0b[:], in_=P.dram["g0rep"]), P.wl)
        E.inc(E.dma_start(out=self.be0b[:], in_=P.dram["be0rep"]), P.wl)

    def e_sync(self, E):
        P = self.P
        E.inc(E.dma_start(out=self.bq1[:], in_=P.dram["bqkv1"]), P.ld)

        for g in range(NG2):
            if g >= 1:
                E.wait(P.tp, self.b("tp") + g * self.TP_G - 4)
            E.inc(E.dma_start(out=self.axt[:],
                              in_=P.dram["xn0"][:, 4 * g:4 * g + 4, :]), P.ld)
            E.wait(P.te, self.b("te") + g * self.TE_G + 8)
            E.inc(E.dma_start(out=P.dram["ved"][:, g], in_=self.vf[:]), P.st)
        for g in range(NG2):
            if g == 0:
                E.wait(P.st, self.b("st") + 16 * NG2)
            else:
                E.wait(P.pe1, self.b("pe1") + NG2 * self.PE1_G + g * self.BPE1_G)
            E.inc(E.dma_start(out=self.vf[:], in_=P.dram["ved"][:, g]), P.ld)
            E.inc(E.dma_start(out=self.bxt[:],
                              in_=P.dram["x_in"][:, 4 * g:4 * g + 4, :]), P.ld)
            E.inc(E.dma_start(out=self.y0g[:],
                              in_=P.dram["y0"][:, 4 * g:4 * g + 4, :]), P.ld)
            if self.P.mode_io.get("adump") == "out":
                E.wait(P.ev1, self.b("ev1") + NG2 * self.EV1_G
                       + g * self.BEV1_G + NDC)
                E.inc(E.dma_start(out=P.dram["adump"][:, g], in_=self.af[:]), P.st)
            for sub in range(4):
                n = 4 * g + sub
                E.wait(P.fin, self.b("fin") + 2 * n + 1)
                E.inc(E.dma_start(out=P.dram["x1"][:, n, :],
                                  in_=self.scrA[:, sub % 2, :]), P.st)
                E.wait(P.fin, self.b("fin") + 2 * n + 2)
                E.inc(E.dma_start(out=P.dram["xh1"][:, n, :],
                                  in_=self.xhst[:, sub % 2, :]), P.st)
        E.inc(E.dma_start(out=P.dram["stats"][:, 0:8, :],
                          in_=P.sb["stat"][:, 0:8, :]), P.st)
        nst_ = NG2 + 2 * NT + 1 + NG2 * self.stg()
        if self.P.mode_io.get("sdump") == "out":
            E.wait(P.gc, self.b("gc") + 2)
            E.inc(E.dma_start(out=P.dram["sdump"][:], in_=self.S32[:]), P.st)
            nst_ += 1
        E.wait(P.st, self.b("st") + 16 * nst_)

    def pc_release(self, u_glob):
        g, r = divmod(u_glob, self.PE2_G)
        if r < 16:
            return ("ev2", self.b("ev2") + g * self.EV2_G + r + 1)
        return ("ev1", self.b("ev1") + g * self.EV1_G + 24 + (r - 16) // 2 + 1)

    def e_tensor(self, E):
        P = self.P
        ones = P.sb["ones_bf"][:, 0:128]
        ident = P.sb["ident"][:, 0:128]
        pT = [P.ps["pT0"], P.ps["pT1"]]
        psA, psB, psC, psD = (P.ps["psA"], P.ps["psB"], P.ps["psC"], P.ps["psD"])
        pC = [psC[:, 0:512], psC[:, 512:1024]]
        E.wait(P.ld, self.b("ld") + 16 * 2)
        E.wait(P.wl, self.b("wl") + 16 * 11)
        first_S = [True, True]
        for g in range(NG2):
            tpb = self.b("tp") + g * self.TP_G
            teb = self.b("te") + g * self.TE_G
            pe1b = self.b("pe1") + g * self.PE1_G
            ev1b = self.b("ev1") + g * self.EV1_G
            pe2b = self.b("pe2") + g * self.PE2_G
            ev2b = self.b("ev2") + g * self.EV2_G
            E.wait(P.ld, self.b("ld") + 16 * (1 + g + 1))
            for sub in range(4):
                ks = g * self.TP_G + sub
                if ks >= 2:
                    E.wait(P.te, self.b("te") + ks - 1)
                for c in range(NDC):
                    mm = E.op("transpose", pT[ks % 2][:, c * 128:(c + 1) * 128],
                              self.axt[:, sub, c * 128:(c + 1) * 128], ident)
                E.inc(mm, P.tp)
            for pi, (wi, w2i) in enumerate(((WQ, WQ2), (WK, WK2), (WV, WV2))):
                E.wait(P.te, teb + 4)
                for j in range(NDC):
                    u = g * self.PE1_G + pi * 8 + j
                    bank = psA if u % 2 == 0 else psB
                    if u >= 2:
                        gg, rr = divmod(u - 2, self.PE1_G)
                        E.wait(P.ev1, self.b("ev1") + gg * self.EV1_G + rr + 1)
                    for c in range(NDC):
                        mm = E.op("matmul", bank[:],
                                  self.w7[:, wi, c, j * 128:(j + 1) * 128],
                                  self.axf[:, c, :],
                                  start=(c == 0), stop=(c == NDC - 1))
                    E.inc(mm, P.pe1)
                for sub in range(4):
                    for hf in range(2):
                        u2 = g * self.PE2_G + pi * 8 + sub * 2 + hf
                        bank = pC[u2 % 2]
                        if u2 >= 2:
                            sem, val = self.pc_release(u2 - 2)
                            E.wait(P.sc[sem] if False else getattr(P, sem), val)
                        E.wait(P.ev1, ev1b + pi * 8 + NDC)
                        for c in range(NDC):
                            mm = E.op("matmul", bank,
                                      self.p1f[:, c, sub * 128:(sub + 1) * 128],
                                      self.w7[:, w2i, c, hf * 512:(hf + 1) * 512],
                                      start=(c == 0), stop=False)
                        mm = E.op("matmul", bank, ones,
                                  self.rowsA[0:1, pi, hf * 512:(hf + 1) * 512],
                                  start=False, stop=True)
                        E.inc(mm, P.pe2)
            for sub in range(4):
                E.wait(P.ev2, ev2b + 8 + 2 * sub + 2)
                for h in range(NHEAD):
                    mm = E.op("matmul",
                              psD[(h % 2) * 64:(h % 2) * 64 + 64,
                                  (h // 2) * 64:(h // 2) * 64 + 64],
                              self.ke[:, sub, h * 64:(h + 1) * 64],
                              self.qe[:, sub, h * 64:(h + 1) * 64],
                              start=first_S[h % 2],
                              stop=(g == NG2 - 1 and sub == 3
                                    and h >= NHEAD - 2),
                              skip_group_check=True,
                              tile_position=(0, (h % 2) * 64))
                    first_S[h % 2] = False
                E.inc(mm, P.pg)
            for sub in range(4):
                ks = g * self.TP_G + 4 + sub
                E.wait(P.ev1, ev1b + 24 + sub + 1)
                if ks >= 2:
                    E.wait(P.te, self.b("te") + ks - 1)
                for c in range(NDC):
                    mm = E.op("transpose", pT[ks % 2][:, c * 128:(c + 1) * 128],
                              self.vsc[:, sub, c * 128:(c + 1) * 128], ident)
                E.inc(mm, P.tp)
        ld0 = self.b("ld") + 16 * (1 + NG2)
        pe1B = self.b("pe1") + NG2 * self.PE1_G
        ev1B = self.b("ev1") + NG2 * self.EV1_G
        pe2B = self.b("pe2") + NG2 * self.PE2_G
        ev2B = self.b("ev2") + NG2 * self.EV2_G
        E.wait(P.gc, self.b("gc") + 2)
        for g in range(NG2):
            E.wait(P.ld, ld0 + 16 * (3 * g + 3))
            for cc in range(NDC):
                u = g * self.BPE1_G + cc
                bank = psA if u % 2 == 0 else psB
                if u >= 2:
                    E.wait(P.ev1, ev1B + u - 1)
                for hh in range(2):
                    h = 2 * cc + hh
                    mm = E.op("matmul",
                              bank[(h % 2) * 64:(h % 2) * 64 + 64, :],
                              self.Sbf[(h % 2) * 64:(h % 2) * 64 + 64,
                                       (h // 2) * 64:(h // 2) * 64 + 64],
                              self.vf[(h % 2) * 64:(h % 2) * 64 + 64, h // 2, :],
                              start=True, stop=True,
                              tile_position=((h % 2) * 64, (h % 2) * 64))
                E.inc(mm, P.pe1)
            for sub in range(4):
                for hf in range(2):
                    u2 = g * self.BPE2_G + sub * 2 + hf
                    bank = pC[u2 % 2]
                    if u2 < 2:
                        E.wait(P.ev1, ev1B)
                    else:
                        E.wait(P.ev2, ev2B + u2 - 1)
                    E.wait(P.ev1, ev1B + g * self.BEV1_G + NDC)
                    for c in range(NDC):
                        mm = E.op("matmul", bank,
                                  self.af[:, c, sub * 128:(sub + 1) * 128],
                                  self.w7[:, WO, c, hf * 512:(hf + 1) * 512],
                                  start=(c == 0), stop=False)
                    mm = E.op("matmul", bank, ones,
                              self.borow[0:1, 0, hf * 512:(hf + 1) * 512],
                              start=False, stop=True)
                    E.inc(mm, P.pe2)

    def e_scalar(self, E):
        P = self.P
        for g in range(NG2):
            pe1b = self.b("pe1") + g * self.PE1_G
            pe2b = self.b("pe2") + g * self.PE2_G
            for pi in range(3):
                for j in range(NDC):
                    u = pi * 8 + j
                    E.wait(P.pe1, pe1b + u + 1)
                    bank = P.ps["psA"] if (g * self.PE1_G + u) % 2 == 0 else P.ps["psB"]
                    E.inc(E.op("activation", self.p1f[:, j, :], bank[:],
                               AF.Identity, scale=1.0,
                               bias=self.bq1[:, pi, j:j + 1]), P.ev1)
                if pi < 2:
                    for sub in range(4):
                        for hf in range(2):
                            kk = g * self.AX_G + pi * 8 + sub * 2 + hf
                            E.wait(P.ax, self.b("ax") + kk + 1)
                            E.op("activation", self.eluA[:], self.eluA[:], AF.Exp)
                            E.incd(P.ge)
            for sub in range(4):
                E.wait(P.gv, self.b("gv") + g * self.GV_G + sub + 1)
                E.wait(P.pe2, pe2b + 16 + sub * 2 + 2)
                if g >= 1:
                    E.wait(P.tp, self.b("tp") + g * self.TP_G)
                for h in range(NHEAD):
                    hf = (h * 64) // 512
                    bi = (16 + sub * 2 + hf) % 2
                    bank = P.ps["psC"][:, 0:512] if bi == 0 else P.ps["psC"][:, 512:1024]
                    mm = E.op("activation",
                              self.vsc[:, sub, h * 64:(h + 1) * 64],
                              bank[:, (h * 64) % 512:(h * 64) % 512 + 64],
                              AF.Identity, scale=self.qksr[:, sub, h:h + 1])
                E.inc(mm, P.ev1)
        axB = self.b("ax") + NG2 * self.AX_G
        geB = self.b("ge") + NG2 * self.GE_G
        ld0 = self.b("ld") + 16 * (1 + NG2)
        for g in range(NG2):
            for sub in range(4):
                n = 4 * g + sub
                E.wait(P.ax, axB + 3 * n + 1)
                E.op("activation", self.st16[:, 1:2], self.st16[:, 0:1],
                     AF.Sqrt, bias=P.cf(EPSG))
                E.incd(P.ge)
                E.wait(P.ld, ld0 + 16 * (3 * g + 3))
                if n >= 2:
                    E.wait(P.st, self.st_x1(n) - 16)
                E.op("activation", self.scrA[:, sub % 2, :],
                     self.y0g[:, sub, :], AF.Exp, scale=P.cf(-1.0))
                E.incd(P.ge)
                E.wait(P.ax, axB + 3 * n + 2)
                E.op("activation", self.st16[:, 6:7], self.st16[:, 5:6],
                     AF.Sqrt, bias=P.cf(EPS0))
                E.incd(P.ge)
                E.wait(P.ax, axB + 3 * n + 3)
                E.op("activation", self.st16[:, 9:10], self.st16[:, 8:9],
                     AF.Sqrt, bias=P.cf(EPS0))
                E.dr()
                E.op("activation", stat(P, ST_SA, n), self.st16[:, 9:10],
                     AF.Identity)
                E.incd(P.ge)

    def e_vector(self, E):
        P = self.P
        pT = [P.ps["pT0"], P.ps["pT1"]]
        psC, psD = P.ps["psC"], P.ps["psD"]
        pC = [psC[:, 0:512], psC[:, 512:1024]]
        for g in range(NG2):
            tpb = self.b("tp") + g * self.TP_G
            pe2b = self.b("pe2") + g * self.PE2_G
            if g >= 1:
                E.wait(P.pg, self.b("pg") + g * self.PG_G)
                E.wait(P.ev1, self.b("ev1") + g * self.EV1_G)
                E.wait(P.st, self.b("st") + 16 * g)
            for sub in range(4):
                ks = g * self.TP_G + sub
                E.wait(P.tp, self.b("tp") + ks + 1)
                E.inc(E.op("tensor_copy",
                           self.axf[:, :, sub * 128:(sub + 1) * 128],
                           pT[ks % 2].rearrange("p (c w) -> p c w", c=NDC)), P.te)
            for pi in range(2):
                dst = self.qe if pi == 0 else self.ke
                for sub in range(4):
                    for hf in range(2):
                        u2 = pi * 8 + sub * 2 + hf
                        kk = g * self.AX_G + u2
                        E.wait(P.pe2, pe2b + u2 + 1)
                        bank = pC[(g * self.PE2_G + u2) % 2]
                        E.op("tensor_scalar_min", self.eluA[:], bank, 0.0)
                        E.inc(E.op("tensor_scalar_max", self.eluB[:], bank, 0.0),
                              P.ax)
                        E.wait(P.ge, self.b("ge") + kk + 1)
                        E.inc(E.op("tensor_add",
                                   dst[:, sub, hf * 512:(hf + 1) * 512],
                                   self.eluA[:], self.eluB[:]), P.ev2)
            for sub in range(4):
                E.op("tensor_mul", self.qk[:], self.qe[:, sub, :], self.ke[:, sub, :])
                E.op("reduce_sum", self.qksr[:, sub, :],
                     self.qk.rearrange("p (h w) -> p h w", h=16), axis=AX.X)
                E.op("tensor_scalar_add", self.qksr[:, sub, :],
                     self.qksr[:, sub, :], EPSQK)
                E.inc(E.op("reciprocal", self.qksr[:, sub, :],
                           self.qksr[:, sub, :]), P.gv)
            for sub in range(4):
                ks = g * self.TP_G + 4 + sub
                E.wait(P.tp, self.b("tp") + ks + 1)
                E.inc(E.op("tensor_copy",
                           self.vf[:, :, sub * 128:(sub + 1) * 128],
                           pT[ks % 2].rearrange("p (c w) -> p c w", c=NDC)), P.te)
        E.wait(P.pg, self.b("pg") + NG2 * self.PG_G)
        E.inc(E.op("tensor_copy", self.S32[:], psD[:, 0:512]), P.gc)
        E.inc(E.op("tensor_copy", self.Sbf[:], self.S32[:]), P.gc)
        axB = self.b("ax") + NG2 * self.AX_G
        geB = self.b("ge") + NG2 * self.GE_G
        ev1B = self.b("ev1") + NG2 * self.EV1_G
        pe1B = self.b("pe1") + NG2 * self.PE1_G
        pe2B = self.b("pe2") + NG2 * self.PE2_G
        ev2B = self.b("ev2") + NG2 * self.EV2_G
        st16 = self.st16
        for g in range(NG2):
            for cc in range(NDC):
                u = g * self.BPE1_G + cc
                E.wait(P.pe1, pe1B + u + 1)
                bank = P.ps["psA"] if u % 2 == 0 else P.ps["psB"]
                E.inc(E.op("tensor_copy", self.af[:, cc, :], bank[:]), P.ev1)
            for sub in range(4):
                n = 4 * g + sub
                p = sub % 2
                yt = self.qk
                for hf in range(2):
                    u2 = g * self.BPE2_G + sub * 2 + hf
                    E.wait(P.pe2, pe2B + u2 + 1)
                    bank = pC[u2 % 2]
                    E.op("scalar_tensor_tensor",
                         yt[:, hf * 512:(hf + 1) * 512],
                         self.bxt[:, sub, hf * 512:(hf + 1) * 512],
                         stat(P, ST_R0, n), bank,
                         op0=ALU.mult, op1=ALU.add)
                    E.inc(E.op("tensor_scalar",
                               yt[:, hf * 512:(hf + 1) * 512],
                               yt[:, hf * 512:(hf + 1) * 512],
                               stat(P, ST_NB0, n), None, ALU.add), P.ev2)
                E.op("reduce_sum", st16[:, 12:13], yt[:], axis=AX.X)
                if n >= 2:
                    E.wait(P.st, self.st_x1(n) - 16)
                E.op("tensor_mul", self.scrB[:, p, :], yt[:], yt[:])
                E.op("reduce_sum", st16[:, 13:14], self.scrB[:, p, :], axis=AX.X)
                E.op("tensor_scalar_mul", st16[:, 14:15], st16[:, 12:13], 1.0 / D)
                E.op("tensor_scalar_mul", st16[:, 13:14], st16[:, 13:14], 1.0 / D)
                E.op("tensor_mul", st16[:, 15:16], st16[:, 14:15], st16[:, 14:15])
                E.inc(E.op("tensor_sub", st16[:, 0:1], st16[:, 13:14],
                           st16[:, 15:16]), P.ax)
                E.wait(P.ge, geB + 4 * n + 1)
                E.op("reciprocal", st16[:, 2:3], st16[:, 1:2])
                E.op("tensor_scalar", st16[:, 3:4], st16[:, 14:15],
                     st16[:, 2:3], -1.0, ALU.mult, ALU.mult)
                E.op("tensor_scalar", yt[:], yt[:], st16[:, 2:3], st16[:, 3:4],
                     ALU.mult, ALU.add)
                E.op("tensor_mul", yt[:], yt[:], self.g0b[:])
                E.op("tensor_add", yt[:], yt[:], self.be0b[:])
                E.wait(P.ge, geB + 4 * n + 2)
                E.op("tensor_scalar_add", self.scrA[:, p, :], self.scrA[:, p, :], 1.0)
                E.op("reciprocal", self.scrA[:, p, :], self.scrA[:, p, :])
                E.op("tensor_mul", self.scrA[:, p, :], self.scrA[:, p, :],
                     self.y0g[:, sub, :])
                E.op("tensor_mul", yt[:], yt[:], self.scrA[:, p, :])
                E.inc(E.op("tensor_add", self.scrA[:, p, :], yt[:],
                           self.bxt[:, sub, :]), P.fin)
                E.op("reduce_sum", st16[:, 4:5], self.scrA[:, p, :], axis=AX.X)
                E.op("tensor_mul", self.scrB[:, p, :], self.scrA[:, p, :],
                     self.scrA[:, p, :])
                E.op("reduce_sum", st16[:, 5:6], self.scrB[:, p, :], axis=AX.X)
                E.op("tensor_scalar_mul", st16[:, 15:16], st16[:, 4:5], 1.0 / D)
                E.op("tensor_scalar_mul", st16[:, 5:6], st16[:, 5:6], 1.0 / D)
                E.op("tensor_mul", st16[:, 4:5], st16[:, 15:16], st16[:, 15:16])
                E.inc(E.op("tensor_sub", st16[:, 5:6], st16[:, 5:6],
                           st16[:, 4:5]), P.ax)
                E.wait(P.ge, geB + 4 * n + 3)
                E.op("reciprocal", stat(P, ST_R1, n), st16[:, 6:7])
                E.op("tensor_scalar", stat(P, ST_NB1, n), st16[:, 15:16],
                     stat(P, ST_R1, n), -1.0, ALU.mult, ALU.mult)
                E.op("tensor_scalar", self.scrB[:, p, :], self.scrA[:, p, :],
                     stat(P, ST_R1, n), stat(P, ST_NB1, n), ALU.mult, ALU.add)
                E.op("reduce_sum", st16[:, 7:8], self.scrB[:, p, :], axis=AX.X)
                E.op("tensor_mul", yt[:], self.scrB[:, p, :], self.scrB[:, p, :])
                E.op("reduce_sum", st16[:, 8:9], yt[:], axis=AX.X)
                E.op("tensor_scalar_mul", st16[:, 10:11], st16[:, 7:8], 1.0 / D)
                E.op("tensor_scalar_mul", st16[:, 8:9], st16[:, 8:9], 1.0 / D)
                E.op("tensor_mul", st16[:, 7:8], st16[:, 10:11], st16[:, 10:11])
                E.inc(E.op("tensor_sub", st16[:, 8:9], st16[:, 8:9],
                           st16[:, 7:8]), P.ax)
                E.wait(P.ge, geB + 4 * n + 4)
                E.op("reciprocal", stat(P, ST_RA, n), st16[:, 9:10])
                E.op("tensor_scalar", stat(P, ST_NBA, n), st16[:, 10:11],
                     stat(P, ST_RA, n), -1.0, ALU.mult, ALU.mult)
                E.inc(E.op("tensor_scalar", self.xhst[:, p, :],
                           self.scrB[:, p, :], stat(P, ST_RA, n),
                           stat(P, ST_NBA, n), ALU.mult, ALU.add), P.fin)


class PH4(Phase):
    name = "ph4"
    TP_G = 20; TE_G = 20; PE1_G = 32; EV1_G = 24
    PE2_G = 8; PG_G = 8; GC_G = 8; EV2_G = 12
    AX_G = 12; GE_G = 12; EP_G = 8; AFF_G = 4; GV_G = 1; FIN_G = 4
    NG = NG2
    DBG = frozenset()

    def setup(self):
        P = self.P
        self.wqkv = P.sbuf("ph4_wqkv", SB_XF, [128, NDC, 3 * D], BF16)
        self.wout = P.sbuf("ph4_wout", SB_Y, [128, NDC, D], BF16)
        self.l1gb = P.sbuf("ph4_l1gb", SB_Y + 16384, [128, D], BF16)
        self.l1bb = P.sbuf("ph4_l1bb", SB_Y + 18432, [128, D], BF16)
        self.borow = P.sbuf("ph4_borow", SB_Y + 20480, [1, 1, D], BF16)
        self.qkf = P.sbuf("ph4_qkf", SB_WA, [128, 24, GS], BF16)
        self.scf = P.sbuf("ph4_scf", SB_WA + 24576, [128, 8, 128], F32)
        self.atb = P.sbuf("ph4_atb", SB_WA + 28672, [128, 8, 128], BF16)
        self.atT = P.sbuf("ph4_atT", SB_WA + 30720, [128, 8, 128], BF16)
        self.xht = P.sbuf("ph4_xht", SB_WB, [128, 4, D], BF16)
        self.h2t = P.sbuf("ph4_h2t", SB_WB + 8192, [128, 4, D], BF16)
        self.hf = P.sbuf("ph4_hf", SB_WB + 16384, [128, NDC, GS], BF16)
        self.vt = P.sbuf("ph4_vt", SB_WB + 24576, [128, 4, D], BF16)
        self.obt = P.sbuf("ph4_obt", SB_FREE, [128, 4, D], BF16)
        self.obf = P.sbuf("ph4_obf", SB_FREE + 8192, [128, NDC, GS], BF16)
        self.xa_t = P.sbuf("ph4_xat", SB_FREE + 16384, [128, 2, D], F32)
        self.bxt = P.sbuf("ph4_bxt", SB_FREE + 24576, [128, 4, D], F32)
        self.mask = P.sbuf("ph4_mask", SB_FREE + 40960, [128, 8, 128], BF16)
        self.mask2 = P.sbuf("ph4_mask2", SB_FREE + 43008, [128, 8, 128], BF16)
        self.ktail = P.sbuf("ph4_ktail", SB_FREE + 45056, [128, 8, 64], BF16)
        self.vtail = P.sbuf("ph4_vtail", SB_FREE + 46080, [128, D], BF16)
        self.bqs = P.sbuf("ph4_bqs", SB_FREE + 48128, [128, 128], F32)
        self.bq8 = P.sbuf("ph4_bq8", SB_FREE + 48640, [128, 8], F32)
        self.nm = P.sbuf("ph4_nm", SB_FREE + 48704, [128, 8], F32)
        self.rr = P.sbuf("ph4_rr", SB_FREE + 48768, [128, 8], F32)
        self.stb = P.sbuf("ph4_stb", SB_FREE + 48832, [128, 16], F32)
        self.yt2 = P.sbuf("ph4_yt2", SB_FREE + 49408, [128, D], F32)
        self.vw = P.sbuf("ph4_vw", SB_FREE + 53504, [128, D], BF16)
        mo = np.zeros((128, 8, 128), np.float32)
        me = np.zeros((128, 8, 128), np.float32)
        for p in range(128):
            qi = p % 64
            for j in range(128):
                if j >= 64 and (j - 64) > qi:
                    mo[p, :, j] = -1e30
                if j < 64 and j > qi:
                    me[p, :, j] = -1e30
        self.c_mask = P.const("c_mask_ph4o", mo.astype(BF))
        self.c_mask2 = P.const("c_mask_ph4e", me.astype(BF))

    def e_gpsimd(self, E):
        P = self.P
        E.inc(E.dma_start(out=self.wqkv[:], in_=P.dram["wqkv"]), P.wl)
        E.inc(E.dma_start(out=self.wout[:], in_=P.dram["wout1"]), P.wl)
        E.inc(E.dma_start(out=self.l1gb[:], in_=P.dram["l1grep"]), P.wl)
        E.inc(E.dma_start(out=self.l1bb[:], in_=P.dram["l1brep"]), P.wl)
        E.inc(E.dma_start(out=self.borow[:], in_=P.dram["rows1"][:, 0:1, :]), P.wl)

    def e_sync(self, E):
        P = self.P
        E.inc(E.dma_start(out=self.bqs[:], in_=P.dram["bqkv"]), P.ld)
        E.inc(E.dma_start(out=self.mask[:], in_=self.c_mask), P.ld)
        E.inc(E.dma_start(out=self.mask2[:], in_=self.c_mask2), P.ld)
        for g in range(self.NG):
            if g >= 1:
                E.wait(P.tp, self.b("tp") + g * self.TP_G - 16)
            E.inc(E.dma_start(out=self.xht[:],
                              in_=P.dram["xh1"][:, 4 * g:4 * g + 4, :]), P.ld)
            if g >= 1:
                E.wait(P.ev2, self.b("ev2") + g * self.EV2_G)
            E.inc(E.dma_start(out=self.bxt[:],
                              in_=P.dram["x1"][:, 4 * g:4 * g + 4, :]), P.ld)
            for sub in range(4):
                n = 4 * g + sub
                E.wait(P.fin, self.b("fin") + n + 1)
                E.inc(E.dma_start(out=P.dram["xa"][:, n, :],
                                  in_=self.xa_t[:, sub % 2, :]), P.st)
        if P.mode_io.get("stats") != "in":
            E.wait(P.fin, self.b("fin") + 4 * self.NG + 1)
            E.inc(E.dma_start(out=P.dram["stats"][:, 0:10, :],
                              in_=P.sb["stat"][:, 0:10, :]), P.st)
            E.wait(P.st, self.b("st") + 16 * (4 * self.NG + 1))
        else:
            E.wait(P.st, self.b("st") + 16 * 4 * self.NG)

    def st_xa(self, n):
        return self.b("st") + 16 * (n + 1)

    def e_tensor(self, E):
        P = self.P
        ones = P.sb["ones_bf"][:, 0:128]
        ident = P.sb["ident"][:, 0:128]
        pT = [P.ps["pT0"], P.ps["pT1"]]
        psA, psB, psC, psD = (P.ps["psA"], P.ps["psB"], P.ps["psC"], P.ps["psD"])
        pC = [psC[:, 0:512], psC[:, 512:1024]]
        E.wait(P.wl, self.b("wl") + 16 * 5)
        E.wait(P.ld, self.b("ld") + 16 * 5)
        for g in range(self.NG):
            tpb = self.b("tp") + g * self.TP_G
            teb = self.b("te") + g * self.TE_G
            pe1b = self.b("pe1") + g * self.PE1_G
            ev1b = self.b("ev1") + g * self.EV1_G
            pe2b = self.b("pe2") + g * self.PE2_G
            pgb = self.b("pg") + g * self.PG_G
            gcb = self.b("gc") + g * self.GC_G
            ev2b = self.b("ev2") + g * self.EV2_G
            epb = self.b("ep") + g * self.EP_G
            for sub in range(4):
                ks = g * self.TP_G + sub
                E.wait(P.aff, self.b("aff") + g * self.AFF_G + sub + 1)
                if ks >= 2:
                    E.wait(P.te, self.b("te") + self.te_before(ks - 1))
                for c in range(NDC):
                    mm = E.op("transpose", pT[ks % 2][:, c * 128:(c + 1) * 128],
                              self.h2t[:, sub, c * 128:(c + 1) * 128], ident)
                E.inc(mm, P.tp)
            E.wait(P.te, teb + 4)
            for j in range(24):
                u = g * self.PE1_G + j
                bank = psA if u % 2 == 0 else psB
                if u >= 2:
                    E.wait(P.ev1, self.ev1_release(u - 2))
                for c in range(NDC):
                    mm = E.op("matmul", bank[:],
                              self.wqkv[:, c, j * 128:(j + 1) * 128],
                              self.hf[:, c, :],
                              start=(c == 0), stop=(c == NDC - 1))
                E.inc(mm, P.pe1)
            for sub in range(4):
                ks = g * self.TP_G + 4 + sub
                E.wait(P.ev1, ev1b + 24)
                if ks >= 2:
                    E.wait(P.te, self.b("te") + self.te_before(ks - 1))
                for c in range(NDC):
                    mm = E.op("transpose", pT[ks % 2][:, c * 128:(c + 1) * 128],
                              self.qkf[:, 16 + c, sub * 128:(sub + 1) * 128], ident)
                E.inc(mm, P.tp)
            for b in range(8):
                bg = g * 8 + b
                if bg >= 1:
                    E.wait(P.gc, self.b("gc") + bg)
                E.wait(P.ev1, ev1b + 16)
                if b == 0:
                    E.wait(P.gv, self.b("gv") + (1 if g == 0 else g + 1))
                for h in range(NHEAD):
                    hb = (h % 2) * 64
                    lhs = self.qkf[hb:hb + 64, h // 2, b * 64:(b + 1) * 64]
                    if b == 0:
                        rhs_prev = self.ktail[hb:hb + 64, h // 2, :]
                    else:
                        rhs_prev = self.qkf[hb:hb + 64, 8 + h // 2,
                                            (b - 1) * 64:b * 64]
                    rhs_cur = self.qkf[hb:hb + 64, 8 + h // 2, b * 64:(b + 1) * 64]
                    if "safescore" in self.DBG:
                        rhs_prev = rhs_cur
                    fo = (h // 2) * 128
                    po, pc_ = (0, 64) if b % 2 == 1 else (64, 0)
                    mm = E.op("matmul", psC[hb:hb + 64, fo + po:fo + po + 64],
                              lhs, rhs_prev, start=True, stop=True,
                              tile_position=(hb, hb))
                    mm = E.op("matmul", psC[hb:hb + 64, fo + pc_:fo + pc_ + 64],
                              lhs, rhs_cur, start=True, stop=True,
                              tile_position=(hb, hb))
                E.inc(mm, P.pe2)
                E.wait(P.ep, epb + b + 1)
                ks = g * self.TP_G + 8 + b
                if True:
                    E.wait(P.te, self.b("te") + self.te_before(ks - 1))
                for hg in range(8):
                    srcT = (self.hf[:, hg, 0:128] if "safeat" in self.DBG
                            else self.atb[:, hg, :])
                    mm = E.op("transpose", pT[ks % 2][:, hg * 128:(hg + 1) * 128],
                              srcT, ident)
                E.inc(mm, P.tp)
                E.wait(P.te, teb + 8 + b + 1)
                if b // 2 >= 1 or g >= 1:
                    prevpair = g * 4 + b // 2 - 1
                    E.wait(P.ev2, self.ev2_obcopy(prevpair))
                cb = (b % 2) * 64
                for h in range(NHEAD):
                    aT = self.atT[:, h // 2, (h % 2) * 64:(h % 2) * 64 + 64]
                    od = psD[cb:cb + 64, h * 64:(h + 1) * 64]
                    rhs = (self.vt[:, b // 2, h * 64:(h + 1) * 64] if b % 2 == 1
                           else self.vw[:, h * 64:(h + 1) * 64])
                    mm = E.op("matmul", od, aT, rhs, start=True, stop=True,
                              tile_position=(0, cb))
                E.inc(mm, P.pg)
            for sub in range(4):
                ks = g * self.TP_G + 16 + sub
                E.wait(P.ev2, ev2b + sub + 1)
                E.wait(P.te, self.b("te") + self.te_before(ks - 1))
                for c in range(NDC):
                    mm = E.op("transpose", pT[ks % 2][:, c * 128:(c + 1) * 128],
                              self.obt[:, sub, c * 128:(c + 1) * 128], ident)
                E.inc(mm, P.tp)
            for sub in range(4):
                for hf_ in range(2):
                    u = g * self.PE1_G + 24 + sub * 2 + hf_
                    bank = psA if u % 2 == 0 else psB
                    r2 = (u - 2) % self.PE1_G
                    if r2 < 24:
                        E.wait(P.ev1, self.b("ev1")
                               + ((u - 2) // self.PE1_G) * self.EV1_G + r2 + 1)
                    else:
                        E.wait(P.ev2, self.b("ev2")
                               + ((u - 2) // self.PE1_G) * self.EV2_G
                               + 4 + (r2 - 24) + 1)
                    E.wait(P.te, teb + 16 + sub + 1)
                    wsrc = self.hf if "safewout" in self.DBG else self.obf
                    for c in range(NDC):
                        mm = E.op("matmul", bank[:],
                                  wsrc[:, c, sub * 128:(sub + 1) * 128],
                                  self.wout[:, c, hf_ * 512:(hf_ + 1) * 512],
                                  start=(c == 0), stop=False)
                    mm = E.op("matmul", bank[:], ones,
                              self.borow[0:1, 0, hf_ * 512:(hf_ + 1) * 512],
                              start=False, stop=True)
                    E.inc(mm, P.pe1)

    def te_before(self, ks):
        g, r = divmod(ks, self.TP_G)
        return g * self.TE_G + r + 1

    def ev1_release(self, u):
        g, r = divmod(u, self.PE1_G)
        if r < 24:
            return self.b("ev1") + g * self.EV1_G + r + 1
        return self.b("ev1") + g * self.EV1_G + 24

    def ev2_obcopy(self, pair):
        g, p = divmod(pair, 4)
        return self.b("ev2") + g * self.EV2_G + p + 1

    def e_scalar(self, E):
        P = self.P
        for g in range(self.NG):
            pe1b = self.b("pe1") + g * self.PE1_G
            axb = self.b("ax") + g * self.AX_G
            for j in range(24):
                u = g * self.PE1_G + j
                E.wait(P.pe1, pe1b + j + 1)
                if g >= 1:
                    E.wait(P.gv, self.b("gv") + g + 1)
                bank = P.ps["psA"] if u % 2 == 0 else P.ps["psB"]
                if j < 8:
                    bias = self.bq8[:, j:j + 1]
                    sc = P.cf(ATT_SCALE)
                else:
                    bias = self.bqs[:, j:j + 1]
                    sc = 1.0
                E.inc(E.op("activation", self.qkf[:, j, :], bank[:],
                           AF.Identity, scale=sc, bias=bias), P.ev1)
            for b in range(8):
                E.wait(P.ax, axb + b + 1)
                if g * 8 + b >= 1:
                    E.wait(P.tp, self.b("tp") + self.tp_at(g * 8 + b - 1))
                fexp = AF.Identity if "safeexp" in self.DBG else AF.Exp
                for hg in range(8):
                    mm = E.op("activation", self.atb[:, hg, :],
                              self.scf[:, hg, :], fexp,
                              bias=self.nm[:, hg:hg + 1])
                E.incd(P.ge)
            for sub in range(4):
                n = 4 * g + sub
                E.wait(P.ax, axb + 8 + sub + 1)
                E.inc(E.op("activation", self.stb[:, 6:7], self.stb[:, 5:6],
                           AF.Sqrt, bias=P.cf(EPS0)), P.ge)

    def tp_at(self, bg):
        g, b = divmod(bg, 8)
        return g * self.TP_G + 8 + b + 1

    def e_vector(self, E):
        P = self.P
        pT = [P.ps["pT0"], P.ps["pT1"]]
        psA, psB, psC, psD = (P.ps["psA"], P.ps["psB"], P.ps["psC"], P.ps["psD"])
        E.wait(P.ld, self.b("ld") + 16 * 5)
        E.wait(P.wl, self.b("wl") + 16 * 5)
        E.op("tensor_scalar_mul", self.bq8[:], self.bqs[:, 0:8], ATT_SCALE)
        E.op("memset", self.ktail[:], 0.0)
        E.inc(E.op("memset", self.vtail[:], 0.0), P.gv)
        for g in range(self.NG):
            teb = self.b("te") + g * self.TE_G
            tpb = self.b("tp") + g * self.TP_G
            pe2b = self.b("pe2") + g * self.PE2_G
            pgb = self.b("pg") + g * self.PG_G
            ev2b = self.b("ev2") + g * self.EV2_G
            axb = self.b("ax") + g * self.AX_G
            geb = self.b("ge") + g * self.GE_G
            epb = self.b("ep") + g * self.EP_G
            E.wait(P.ld, self.b("ld") + 16 * (3 + 2 * (g + 1)))
            for sub in range(4):
                if g >= 1:
                    E.wait(P.tp, tpb - 16)
                E.op("tensor_mul", self.yt2[:], self.xht[:, sub, :], self.l1gb[:])
                E.inc(E.op("tensor_add", self.h2t[:, sub, :], self.yt2[:],
                           self.l1bb[:]), P.aff)
            for sub in range(4):
                ks = g * self.TP_G + sub
                E.wait(P.tp, self.b("tp") + self.tp_done(ks))
                E.inc(E.op("tensor_copy",
                           self.hf[:, :, sub * 128:(sub + 1) * 128],
                           pT[ks % 2].rearrange("p (c w) -> p c w", c=NDC)), P.te)
            for sub in range(4):
                ks = g * self.TP_G + 4 + sub
                E.wait(P.tp, self.b("tp") + self.tp_done(ks))
                E.inc(E.op("tensor_copy",
                           self.vt[:, sub, :].rearrange("p (c w) -> p c w", c=NDC),
                           pT[ks % 2].rearrange("p (c w) -> p c w", c=NDC)), P.te)
            for b in range(8):
                bg = g * 8 + b
                E.wait(P.pe2, pe2b + b + 1)
                mk = self.mask if b % 2 == 1 else self.mask2
                E.inc(E.op("tensor_add", self.scf[:], psC.rearrange(
                    "p (a w) -> p a w", a=8), mk[:]), P.gc)
                E.inc(E.op("reduce_max", self.nm[:], self.scf[:], axis=AX.X,
                           negate=True), P.ax)
                E.wait(P.ge, geb + b + 1)
                E.op("reduce_sum", self.rr[:], self.atb[:], axis=AX.X)
                E.op("reciprocal", self.rr[:], self.rr[:])
                for hg in range(8):
                    mm = E.op("tensor_scalar", self.atb[:, hg, :],
                              self.atb[:, hg, :], self.rr[:, hg:hg + 1],
                              None, ALU.mult)
                E.inc(mm, P.ep)
                if b % 2 == 0:
                    pv = (self.vtail[64:128, :] if b == 0
                          else self.vt[64:128, (b - 1) // 2, :])
                    E.op("tensor_copy", self.vw[64:128, :], pv)
                    E.op("tensor_copy", self.vw[0:64, :], self.vt[0:64, b // 2, :])
                ks = g * self.TP_G + 8 + b
                E.wait(P.tp, self.b("tp") + self.tp_done(ks))
                E.inc(E.op("tensor_copy", self.atT[:],
                           pT[ks % 2].rearrange("p (h w) -> p h w", h=8)), P.te)
                if b % 2 == 1:
                    E.wait(P.pg, pgb + b + 1)
                    E.inc(E.op("tensor_copy", self.obt[:, b // 2, :],
                               psD[:]), P.ev2)
            for sub in range(4):
                ks = g * self.TP_G + 16 + sub
                E.wait(P.tp, self.b("tp") + self.tp_done(ks))
                E.inc(E.op("tensor_copy",
                           self.obf[:, :, sub * 128:(sub + 1) * 128],
                           pT[ks % 2].rearrange("p (c w) -> p c w", c=NDC)), P.te)
            E.op("tensor_copy", self.ktail[:], self.qkf[:, 8:16, 448:512])
            E.inc(E.op("tensor_copy", self.vtail[:], self.vt[:, 3, :]), P.gv)
            for sub in range(4):
                n = 4 * g + sub
                p = sub % 2
                for hf_ in range(2):
                    u = g * self.PE1_G + 24 + sub * 2 + hf_
                    E.wait(P.pe1, self.b("pe1") + u + 1)
                    bank = psA if u % 2 == 0 else psB
                    E.op("scalar_tensor_tensor",
                         self.yt2[:, hf_ * 512:(hf_ + 1) * 512],
                         self.bxt[:, sub, hf_ * 512:(hf_ + 1) * 512],
                         stat(P, ST_R1, n), bank[:],
                         op0=ALU.mult, op1=ALU.add)
                    E.inc(E.op("tensor_scalar",
                               self.yt2[:, hf_ * 512:(hf_ + 1) * 512],
                               self.yt2[:, hf_ * 512:(hf_ + 1) * 512],
                               stat(P, ST_NB1, n), None, ALU.add), P.ev2)
                if n >= 2:
                    E.wait(P.st, self.st_xa(n - 1))
                E.inc(E.op("tensor_copy", self.xa_t[:, p, :], self.yt2[:]), P.fin)
                E.op("reduce_sum", self.stb[:, 4:5], self.yt2[:], axis=AX.X)
                E.op("tensor_mul", self.yt2[:], self.yt2[:], self.yt2[:])
                E.op("reduce_sum", self.stb[:, 5:6], self.yt2[:], axis=AX.X)
                E.op("tensor_scalar_mul", self.stb[:, 4:5], self.stb[:, 4:5], 1.0 / D)
                E.op("tensor_scalar_mul", self.stb[:, 5:6], self.stb[:, 5:6], 1.0 / D)
                E.op("tensor_mul", self.stb[:, 7:8], self.stb[:, 4:5],
                     self.stb[:, 4:5])
                E.inc(E.op("tensor_sub", self.stb[:, 5:6], self.stb[:, 5:6],
                           self.stb[:, 7:8]), P.ax)
                E.wait(P.ge, geb + 8 + sub + 1)
                E.op("reciprocal", stat(P, ST_RXA, n), self.stb[:, 6:7])
                E.op("tensor_scalar", stat(P, ST_NBXA, n), self.stb[:, 4:5],
                     stat(P, ST_RXA, n), -1.0, ALU.mult, ALU.mult)
        E.inc(E.op("memset", self.stb[:, 15:16], 0.0), P.fin)

    def tp_done(self, ks):
        g, r = divmod(ks, self.TP_G)
        return g * self.TP_G + r + 1


def kernel(**inputs):
    ins = {k: np.asarray(v) for k, v in inputs.items()}
    g = host_inputs(ins)
    x = np.ascontiguousarray(ins["L0_attn_x"], dtype=np.float32)
    P = build_program([lambda P: PH0(P), lambda P: MoE(P, "l0"),
                       lambda P: PH2(P), lambda P: MoE(P, "l1"),
                       lambda P: PH4(P), lambda P: MoE(P, "mlp")], {})
    in_maps = []
    for c in range(8):
        m = dict(g)
        m["x_in"] = lay_tm(x[c])
        in_maps.append(m)
    res = run_bass_kernel_spmd(P.nc, in_maps, core_ids=list(range(8)))
    out = np.stack([unlay_tm(res.results[c]["out"]) for c in range(8)])
    return np.ascontiguousarray(out, dtype=np.float32)
